# version 16
# baseline (speedup 1.0000x reference)
import numpy as np
import ml_dtypes
import concourse.bass as bass
import concourse.mybir as mybir
from concourse.bass_utils import run_bass_kernel_spmd
from contextlib import ExitStack

F32 = mybir.dt.float32
BF16 = mybir.dt.bfloat16
U8 = mybir.dt.uint8
AF = mybir.ActivationFunctionType
ALU = mybir.AluOpType

NPHASE = 6
STRICT = False

D = 2048
T = 2048
HID = 8192
LN_EPS = 1e-5

XN_B = 0
QA_B = 65536
WST_B = 131072
XF_B = 163840
SGEV_B = 180224
SHR_B = 188416
STRIP_B = 201728
CST_B = 205824
ARENA_B = 212480

CC_B = WST_B + 24576
KVB_B = WST_B + 28736
KSM_B = WST_B + 30784
RECIP_B = XN_B + 16512
SELT_B = CST_B + 2560


def _had(n):
    h = np.array([[1.0]])
    while h.shape[0] < n:
        h = np.block([[h, h], [h, -h]])
    return h


def _xncov(b0, b1):
    return [f"xn{c}.{t}" for c in range(b0 // 4096, (b1 - 1) // 4096 + 1)
            for t in range(4)]


def build_program(nphase):
    nc = bass.Bass("TRN2", target_bir_lowering=False)

    xin = nc.dram_tensor("xin", [D, T], F32, kind="ExternalInput").ap()
    w1t = nc.dram_tensor("w1t", [D, HID], F32, kind="ExternalInput").ap()
    w3t = nc.dram_tensor("w3t", [D, HID], F32, kind="ExternalInput").ap()
    w2t = nc.dram_tensor("w2t", [HID, D], F32, kind="ExternalInput").ap()
    wq = nc.dram_tensor("wq", [D, D], F32, kind="ExternalInput").ap()
    wkd = nc.dram_tensor("wkd", [D, D], F32, kind="ExternalInput").ap()
    wv = nc.dram_tensor("wv", [D, D], F32, kind="ExternalInput").ap()
    wo = nc.dram_tensor("wo", [D, D], F32, kind="ExternalInput").ap()
    bo_t = nc.dram_tensor("bo_t", [128, 16], F32, kind="ExternalInput").ap()
    hs128 = nc.dram_tensor("hs128", [128, 128], F32, kind="ExternalInput").ap()
    h64d = nc.dram_tensor("h64d", [64, 64], BF16, kind="ExternalInput").ap()
    seld = nc.dram_tensor("seld", [32, 2048], F32, kind="ExternalInput").ap()
    seltd = nc.dram_tensor("seltd", [128, 512], F32, kind="ExternalInput").ap()

    ydr = nc.dram_tensor("ydr", [HID, D], BF16)
    w2s = nc.dram_tensor("w2s", [HID, D], BF16)
    hidd = nc.dram_tensor("hidd", [HID, T], BF16)
    x1d = nc.dram_tensor("x1d", [D, T], F32)
    x2d = nc.dram_tensor("x2d", [D, T], F32)
    cci = nc.dram_tensor("cci", [128, 1040], F32)
    cco = nc.dram_tensor("cco", [128, 1040], F32)
    out = nc.dram_tensor("out", [D, T], F32, kind="ExternalOutput").ap()

    es = ExitStack()
    arena = es.enter_context(nc.sbuf_tensor("arena", [128, ARENA_B], U8))
    a_f32 = arena.bitcast(F32).ap()
    a_bf = arena.bitcast(BF16).ap()

    def f32c(boff, rows, cols):
        assert boff % 4 == 0 and boff + 4 * cols <= ARENA_B
        return a_f32[0:rows, boff // 4: boff // 4 + cols]

    def bfc(boff, rows, cols):
        assert boff % 2 == 0 and boff + 2 * cols <= ARENA_B
        return a_bf[0:rows, boff // 2: boff // 2 + cols]

    ps = [es.enter_context(nc.psum_tensor(f"ps{i}", [128, 512], F32)) for i in range(8)]
    NDS = 8
    sem_names = ["epe", "eact", "edve", "epool"]
    for q in ("sp", "act", "pool"):
        sem_names += [f"d{q}{j}" for j in range(NDS)]
    sems = {s: es.enter_context(nc.semaphore(s)) for s in sem_names}

    ESEM = {"pe": "epe", "act": "eact", "dve": "edve", "pool": "epool"}

    class Plan:
        def __init__(self):
            self.ops = {e: [] for e in ("sp", "pe", "act", "dve", "pool")}
            self.cnt = {s: 0 for s in sems}
            self.wm = {e: {} for e in self.ops}
            self.lastw = {}
            self.readers = {}
            self.dma_idx = {"sp": 0, "act": 0, "pool": 0}

        def op(self, eng, fn, reads=(), writes=(), dma=False):
            need = {}

            def add(sc):
                if sc is not None and need.get(sc[0], 0) < sc[1]:
                    need[sc[0]] = sc[1]

            if STRICT == "all" or STRICT is True:
                for s, c in self.cnt.items():
                    if c > 0:
                        add((s, c))
            elif STRICT == "dma" and dma:
                for s, c in self.cnt.items():
                    if s[0] == "d" and c > 0:
                        add((s, c))
            elif STRICT == "compute" and not dma:
                for s in ("epe", "eact", "edve", "epool"):
                    if self.cnt[s] > 0:
                        add((s, self.cnt[s]))
            for k in reads:
                add(self.lastw.get(k))
            for k in writes:
                add(self.lastw.get(k))
                for item in (self.readers.get(k) or {}).items():
                    add(item)
            if dma:
                j = self.dma_idx[eng] % NDS
                self.dma_idx[eng] += 1
                sem = f"d{eng}{j}"
                if self.cnt[sem] > 0:
                    add((sem, self.cnt[sem]))
                self.cnt[sem] += 16
            else:
                sem = ESEM[eng]
                self.cnt[sem] += 1
            post = (sem, self.cnt[sem])
            waits = []
            for s, c in sorted(need.items()):
                if eng == "pe" and s == "epe":
                    continue
                if c <= self.wm[eng].get(s, 0):
                    continue
                waits.append((s, c))
                self.wm[eng][s] = c
            for k in writes:
                self.lastw[k] = post
                self.readers[k] = {}
            for k in reads:
                d = self.readers.setdefault(k, {})
                if d.get(post[0], 0) < post[1]:
                    d[post[0]] = post[1]
            self.ops[eng].append((waits, fn, post))

        def run(self, ename, eng):
            for waits, fn, post in self.ops[ename]:
                for s, c in waits:
                    eng.wait_ge(sems[s], c)
                inst = fn(eng)
                inst.then_inc(sems[post[0]], 16 if post[0][0] == "d" else 1)

    P = Plan()

    c_hs = f32c(CST_B, 128, 128)
    c_h64 = bfc(CST_B + 512, 64, 64)
    c_h64b = a_bf[64:128, (CST_B + 640) // 2:(CST_B + 640) // 2 + 64]
    c_1f = f32c(CST_B + 768, 128, 1)
    c_1b = bfc(CST_B + 772, 128, 1)
    c_1r = f32c(CST_B + 784, 1, 128)
    c_bo = f32c(CST_B + 1296, 128, 16)
    c_xsq = [bfc(CST_B + 1360, 128, 256), bfc(CST_B + 1872, 128, 256)]

    P.op("sp", lambda e: e.dma_start(out=c_hs, in_=hs128), writes=["c_hs"], dma=True)
    P.op("sp", lambda e: e.dma_start(out=c_h64, in_=h64d), writes=["c_h64"], dma=True)
    P.op("sp", lambda e: e.dma_start(out=c_h64b, in_=h64d), writes=["c_h64b"], dma=True)
    P.op("sp", lambda e: e.dma_start(out=c_bo, in_=bo_t), writes=["c_bo"], dma=True)
    P.op("pool", lambda e: e.memset(c_1f, 1.0), writes=["c_1f"])
    P.op("pool", lambda e: e.memset(c_1b, 1.0), writes=["c_1b"])
    P.op("pool", lambda e: e.memset(c_1r, 1.0), writes=["c_1r"])

    xfb = bfc(XF_B, 128, 4096)
    xfb3 = xfb.rearrange("p (c t) -> p c t", c=16)
    xlb = bfc(QA_B + 57344, 128, 4096)
    xlb3 = xlb.rearrange("p (c t) -> p c t", c=16)
    xf = f32c(XF_B, 128, 4096)
    sg = [f32c(SGEV_B + j * 2048, 128, 512) for j in range(4)]

    def strip(j):
        return f32c(STRIP_B + j * 1024, 1, 256)

    def xnc(c):
        return bfc(XN_B + c * 4096, 128, 2048)

    def qac(i):
        return bfc(QA_B + i * 4096, 128, 2048)

    def plan_ln(src_dram, src_keys, side=None, nside=0, tiles=range(8)):
        src_r = src_dram.rearrange("(c p) t -> p c t", p=128)
        for tt in tiles:
            for _ in range(nside):
                if side:
                    side.pop(0)()
            t0 = tt * 256
            t4 = tt // 2
            x3 = xfb3 if tt % 2 == 0 else xlb3
            xk = ((lambda c: f"xf{c // 2}") if tt % 2 == 0
                  else (lambda c: f"qa{14 + c // 8}.{(c // 2) % 4}"))
            XALL = sorted({xk(c) for c in range(16)})
            if side is not None:
                bs = [0, 1] if tt % 2 == 0 else [2, 3]
                bb = [2, 3] if tt % 2 == 0 else [0, 1]
            else:
                bs = [0, 1] if tt % 2 == 0 else [2, 3]
                bb = [4, 5] if tt % 2 == 0 else [0, 1]
            P.op("pool", lambda e, t0=t0, x3=x3: e.dma_start(
                out=x3, in_=src_r[:, :, t0:t0 + 256]),
                reads=src_keys(t4), writes=XALL, dma=True)
            def sumx(e, x3=x3, bs=bs):
                last = None
                for c in range(16):
                    last = e.matmul(ps[bs[0]][0:1, 0:256], c_1b, x3[:, c, :],
                                    start=(c == 0), stop=(c == 15),
                                    skip_group_check=True)
                return last
            P.op("pe", sumx, reads=XALL + ["c_1b"], writes=[f"ps{bs[0]}"])
            for c in range(16):
                xq = c_xsq[c % 2]
                xqk = f"xsq{c % 2}"
                P.op("act", lambda e, c=c, xq=xq, x3=x3: e.activation(
                    xq, x3[:, c, :], AF.Square),
                    reads=[xk(c)], writes=[xqk])
                P.op("pe", lambda e, c=c, xq=xq, bs=bs: e.matmul(
                    ps[bs[1]][0:1, 0:256], c_1b, xq,
                    start=(c == 0), stop=(c == 15), skip_group_check=True),
                    reads=[xqk, "c_1b"],
                    writes=[f"ps{bs[1]}"] if c in (0, 15) else [])
            P.op("act", lambda e, bs=bs: e.mul(strip(0), ps[bs[0]][0:1, 0:256], 1.0 / 2048.0),
                 reads=[f"ps{bs[0]}"], writes=["st0"])
            P.op("act", lambda e, bs=bs: e.mul(strip(1), ps[bs[1]][0:1, 0:256], 1.0 / 2048.0),
                 reads=[f"ps{bs[1]}"], writes=["st1"])
            P.op("dve", lambda e: e.tensor_tensor(strip(2), strip(0), strip(0), ALU.mult),
                 reads=["st0"], writes=["st2"])
            P.op("dve", lambda e: e.tensor_sub(strip(1), strip(1), strip(2)),
                 reads=["st1", "st2"], writes=["st1"])
            P.op("dve", lambda e: e.tensor_scalar_add(strip(1), strip(1), LN_EPS),
                 reads=["st1"], writes=["st1"])
            P.op("act", lambda e: e.activation(strip(2), strip(1), AF.Sqrt),
                 reads=["st1"], writes=["st2"])
            P.op("dve", lambda e: e.reciprocal(strip(3), strip(2)),
                 reads=["st2"], writes=["st3"])
            P.op("dve", lambda e: e.tensor_tensor(strip(1), strip(0), strip(3), ALU.mult),
                 reads=["st0", "st3"], writes=["st1"])
            P.op("pe", lambda e, bb=bb: e.matmul(ps[bb[0]][:, 0:256], c_1r, strip(3),
                                                 start=True, stop=True,
                                                 skip_group_check=True),
                 reads=["st3", "c_1r"], writes=[f"ps{bb[0]}"])
            P.op("pe", lambda e, bb=bb: e.matmul(ps[bb[1]][:, 0:256], c_1r, strip(1),
                                                 start=True, stop=True,
                                                 skip_group_check=True),
                 reads=["st1", "c_1r"], writes=[f"ps{bb[1]}"])
            for c in range(16):
                sgj = c % 2
                P.op("dve", lambda e, c=c, sgj=sgj, x3=x3, bb=bb: e.tensor_tensor(
                    sg[sgj][:, 0:256], x3[:, c, :], ps[bb[0]][:, 0:256], ALU.mult),
                    reads=[xk(c), f"ps{bb[0]}"], writes=[f"sg{sgj}"])
                P.op("dve", lambda e, c=c, sgj=sgj, t0=t0, bb=bb: e.tensor_tensor(
                    xnc(c)[:, t0:t0 + 256], sg[sgj][:, 0:256], ps[bb[1]][:, 0:256],
                    ALU.subtract),
                    reads=[f"sg{sgj}", f"ps{bb[1]}"], writes=[f"xn{c}.{t4}"])

    def make_p0_thunks():
        NRING = 20
        DEPTH = 16

        def ring(j):
            return bfc(QA_B + 16384 + j * 2048, 128, 1024)

        def ringk(j):
            return [f"p0r{j}"]

        p0outs = [bfc(XF_B + 8192 + k * 2048, 128, 1024) for k in range(4)]
        p0oks = [[f"xf{8 + 2 * k}", f"xf{9 + 2 * k}"] for k in range(4)]

        def ringf(j):
            return f32c(QA_B + 16384 + j * 4096, 128, 1024)

        loads1, comps1, loads2, comps2 = [], [], [], []
        for cp in range(64):
            for mh in range(2):
                i = cp * 2 + mh
                s = i % (NRING // 2)
                ins = ringf(s)
                ik = [f"p0r{2 * s}", f"p0r{2 * s + 1}"]
                outs = p0outs[i % 4]
                ok = p0oks[i % 4]

                def ld(cp=cp, mh=mh, ins=ins, ik=ik):
                    P.op("act", lambda e: e.dma_start(
                        out=ins,
                        in_=w2t[cp * 128:cp * 128 + 128, mh * 1024:mh * 1024 + 1024]),
                        writes=ik, dma=True)

                def cm(cp=cp, mh=mh, i=i, ins=ins, outs=outs, ik=ik, ok=ok):
                    for q in range(2):
                        b = 4 + 2 * (i % 2) + q
                        P.op("pe", lambda e, q=q, b=b: e.matmul(
                            ps[b][:, :], c_hs, ins[:, q * 512:q * 512 + 512],
                            start=True, stop=True, skip_group_check=True),
                            reads=ik + ["c_hs"], writes=[f"ps{b}"])
                        P.op("dve", lambda e, q=q, b=b: e.tensor_copy(
                            outs[:, q * 512:q * 512 + 512], ps[b][:, :]),
                            reads=[f"ps{b}"], writes=ok)
                    P.op("sp", lambda e: e.dma_start(
                        out=ydr.ap()[cp * 128:cp * 128 + 128,
                                     mh * 1024:mh * 1024 + 1024],
                        in_=outs),
                        reads=ok, writes=[f"ydr{cp}"], dma=True)
                loads1.append(ld)
                comps1.append(cm)
        y_r2 = ydr.ap().rearrange("(c p) m -> p c m", p=128)
        ws_r2 = w2s.ap().rearrange("(c p) m -> p c m", p=128)
        first = [True]
        for dp in range(64):
            for mh in range(2):
                i = dp * 2 + mh
                s = i % NRING
                ins = ring(s)
                ik = ringk(s)
                outs = p0outs[i % 4]
                ok = p0oks[i % 4]

                def ld(dp=dp, mh=mh, ins=ins, ik=ik):
                    rd = ik + ([f"ydr{cp}" for cp in range(64)]
                               if first[0] else [f"ydr{0}"])
                    first[0] = False
                    P.op("sp", lambda e: e.dma_start(
                        out=ins,
                        in_=y_r2[2 * dp:2 * dp + 2, :, mh * 1024:mh * 1024 + 1024]),
                        reads=rd, writes=ik, dma=True)

                def cm(dp=dp, mh=mh, i=i, ins=ins, outs=outs, ik=ik, ok=ok):
                    for q in range(2):
                        b = 4 + 2 * (i % 2) + q
                        def mmp(e, q=q, b=b, ins=ins):
                            e.matmul(
                                ps[b][0:64, :], c_h64,
                                ins[0:64, q * 512:q * 512 + 512],
                                start=True, stop=True, skip_group_check=True)
                            return e.matmul(
                                ps[b][64:128, :], c_h64b,
                                ins[64:128, q * 512:q * 512 + 512],
                                start=True, stop=True, skip_group_check=True)
                        P.op("pe", mmp, reads=ik + ["c_h64", "c_h64b"],
                             writes=[f"ps{b}"])
                        P.op("dve", lambda e, q=q, b=b: e.tensor_copy(
                            outs[:, q * 512:q * 512 + 512], ps[b][:, :]),
                            reads=[f"ps{b}"], writes=ok)
                    P.op("sp", lambda e: e.dma_start(
                        out=ws_r2[2 * dp:2 * dp + 2, :, mh * 1024:mh * 1024 + 1024],
                        in_=outs),
                        reads=ok, writes=["w2s"], dma=True)
                loads2.append(ld)
                comps2.append(cm)

        thunks = []
        for loads, comps, dep in ((loads1, comps1, 8), (loads2, comps2, DEPTH)):
            n = len(loads)
            for k in range(n + dep):
                def th(k=k, loads=loads, comps=comps, n=n, dep=dep):
                    if k < n:
                        loads[k]()
                    if k >= dep:
                        comps[k - dep]()
                thunks.append(th)
        return thunks

    def plan_ffn_up(side):
        w1r = w1t.rearrange("(kc p) h -> p kc h", p=128)
        w3r = w3t.rearrange("(kc p) h -> p kc h", p=128)
        pos = [0]
        for hs in range(32):
            for _ in range(7):
                if side:
                    side.pop(0)()
            slot = hs % 2
            wslab = bfc(WST_B + slot * 16384, 128, 8192)
            ka, kb = f"w{slot}a", f"w{slot}b"
            P.op("pool", lambda e, hs=hs, wslab=wslab: e.dma_start(
                out=wslab[:, 0:4096].rearrange("p (kc h) -> p kc h", kc=16),
                in_=w1r[:, :, hs * 256:hs * 256 + 256]),
                writes=[ka], dma=True)
            P.op("pool", lambda e, hs=hs, wslab=wslab: e.dma_start(
                out=wslab[:, 4096:8192].rearrange("p (kc h) -> p kc h", kc=16),
                in_=w3r[:, :, hs * 256:hs * 256 + 256]),
                writes=[kb], dma=True)
            W = 2 if side is not None else 4
            npos = (4 if side is not None else 8) // W
            for hc in range(2):
                hcg = hs * 2 + hc
                qslot = hcg % 4
                slab = qac(qslot)
                for w0 in range(0, 4, W):
                    gb = {}
                    for off, wkk, mat in ((0, ka, "g"), (4096, kb, "l")):
                        p = pos[0]
                        pos[0] = (pos[0] + 1) % npos
                        banks = [p * W + j for j in range(W)]
                        gb[mat] = banks

                        def mmg(e, off=off, banks=banks, hc=hc, w0=w0, wslab=wslab):
                            last = None
                            for kc in range(16):
                                o = off + kc * 256 + hc * 128
                                for j, b in enumerate(banks):
                                    last = e.matmul(
                                        ps[b][:, :], wslab[:, o:o + 128],
                                        xnc(kc)[:, (w0 + j) * 512:(w0 + j) * 512 + 512],
                                        start=(kc == 0), stop=(kc == 15),
                                        skip_group_check=True)
                            return last
                        P.op("pe", mmg,
                             reads=[wkk] + [f"xn{kc}.{w0 + j}" for kc in range(16)
                                            for j in range(W)],
                             writes=[f"ps{b}" for b in banks])
                    for j in range(W):
                        t4 = w0 + j
                        bg, bl = gb["g"][j], gb["l"][j]
                        sgj = t4 % 4
                        P.op("act", lambda e, bg=bg, sgj=sgj: e.activation(
                            sg[sgj], ps[bg][:, :], AF.Silu),
                            reads=[f"ps{bg}"], writes=[f"sg{sgj}"])
                        P.op("dve", lambda e, bl=bl, sgj=sgj, slab=slab, t4=t4,
                             qslot=qslot: e.tensor_tensor(
                            slab[:, t4 * 512:t4 * 512 + 512], sg[sgj], ps[bl][:, :],
                            ALU.mult),
                            reads=[f"sg{sgj}", f"ps{bl}"], writes=[f"qa{qslot}.{t4}"])
                P.op("sp", lambda e, hcg=hcg, slab=slab: e.dma_start(
                    out=hidd.ap()[hcg * 128:hcg * 128 + 128, :], in_=slab),
                    reads=[f"qa{qslot}.{t}" for t in range(4)],
                    writes=[f"hid{hcg}"], dma=True)

    def plan_ffn_down(resid_dram, out_dram, resid_keys, okey, mid=None):
        hid_r = hidd.ap().rearrange("(hc p) t -> p hc t", p=128)
        ws_r2 = w2s.ap().rearrange("(hc p) m -> p hc m", p=128)

        def cover(hclo, hchi):
            keys = []
            for hc in range(hclo, hchi):
                if hc < 32:
                    keys += [f"xn{hc // 2}.{t}" for t in range(4)]
                else:
                    keys += [f"qa{(hc - 32) // 2}.{t}" for t in range(4)]
            return sorted(set(keys))

        for th in range(2):
            if th == 1 and mid is not None:
                mid()
            for g in range(8):
                dst = a_bf[:, g * 8192: g * 8192 + 8192]
                P.op("sp", lambda e, th=th, g=g, dst=dst: e.dma_start(
                    out=dst.rearrange("p (hc t) -> p hc t", hc=8),
                    in_=hid_r[:, g * 8:g * 8 + 8, th * 1024:th * 1024 + 1024]),
                    reads=[f"hid{hc}" for hc in range(g * 8, g * 8 + 8)] + ["w2s"],
                    writes=cover(g * 8, g * 8 + 8), dma=True)
            for ms in range(8):
                banks = [4 * (ms % 2) + j for j in range(4)]
                for sub in range(2):
                    wsl = (ms * 2 + sub) % 2
                    wslab = bfc(WST_B + wsl * 16384, 128, 8192)
                    wkeys = [f"w{wsl}a", f"w{wsl}b"]
                    rd = ["w2s"] if (th == 0 and ms == 0 and sub == 0) else []
                    P.op("act", lambda e, ms=ms, sub=sub, wslab=wslab: e.dma_start(
                        out=wslab.rearrange("p (hc m) -> p hc m", hc=32),
                        in_=ws_r2[:, sub * 32:sub * 32 + 32, ms * 256:ms * 256 + 256]),
                        reads=rd, writes=wkeys, dma=True)

                    for part in range(4):
                        def mmh(e, sub=sub, part=part, wslab=wslab, banks=banks):
                            last = None
                            for hcl in range(part * 8, part * 8 + 8):
                                hc = sub * 32 + hcl
                                hbuf = a_bf[:, hc * 1024: hc * 1024 + 1024]
                                for mc in range(2):
                                    for t2 in range(2):
                                        last = e.matmul(
                                            ps[banks[mc * 2 + t2]][:, :],
                                            wslab[:, hcl * 256 + mc * 128:
                                                  hcl * 256 + mc * 128 + 128],
                                            hbuf[:, t2 * 512:t2 * 512 + 512],
                                            start=(hc == 0), stop=(hc == 63),
                                            skip_group_check=True)
                            return last
                        P.op("pe", mmh,
                             reads=wkeys + cover(sub * 32 + part * 8,
                                                 sub * 32 + part * 8 + 8),
                             writes=[f"ps{b}" for b in banks] if part in (0, 3)
                                    else [])
                for mc in range(2):
                    for t2 in range(2):
                        j = mc * 2 + t2
                        b = banks[j]
                        mg = ms * 2 + mc
                        t4 = th * 2 + t2
                        r0, c0 = mg * 128, t4 * 512
                        xt = xf[:, j * 512:j * 512 + 512]
                        xk = [f"xf{2 * j}", f"xf{2 * j + 1}"]
                        P.op("act", lambda e, r0=r0, c0=c0, xt=xt, mg=mg, t4=t4:
                             e.dma_start(out=xt,
                                         in_=resid_dram[r0:r0 + 128, c0:c0 + 512]),
                             reads=resid_keys(mg, t4), writes=xk, dma=True)
                        P.op("dve", lambda e, b=b, xt=xt, j=j: e.tensor_tensor(
                            sg[j], ps[b][:, :], xt, ALU.add),
                            reads=[f"ps{b}"] + xk, writes=[f"sg{j}"])
                        P.op("sp", lambda e, r0=r0, c0=c0, j=j: e.dma_start(
                            out=out_dram[r0:r0 + 128, c0:c0 + 512], in_=sg[j]),
                            reads=[f"sg{j}"], writes=[f"{okey}.{mg}.{t4}"], dma=True)

    def plan_attn_a():
        wqr = wq.rearrange("(kc p) i -> p kc i", p=128)
        wkr = wkd.rearrange("(kc p) i -> p kc i", p=128)
        wvr = wv.rearrange("(kc p) i -> p kc i", p=128)
        ktm_s = [bfc(SHR_B + 8192, 128, 256), bfc(SHR_B + 8704, 128, 256)]
        vtm_s = [bfc(SHR_B + 9216, 128, 256), bfc(SHR_B + 9728, 128, 256)]
        wcnt = [0]

        def wkey(j):
            return f"w{j // 2}{'ab'[j % 2]}"

        def wload(si, src):
            j = wcnt[0] % 3
            wcnt[0] += 1
            wsl = bfc(WST_B + j * 8192, 128, 4096)
            P.op("pool", lambda e, si=si, src=src, wsl=wsl: e.dma_start(
                out=wsl.rearrange("p (kc i) -> p kc i", kc=16),
                in_=src[:, :, si * 256:si * 256 + 256]),
                writes=[wkey(j)], dma=True)
            return j

        for si in range(8):
            jk = wload(si, wkr)
            jv = wload(si, wvr)
            for tc in range(16):
                par = tc % 2
                ktm, vtm = ktm_s[par], vtm_s[par]
                for jm, dst, dk, pb, relu in ((jk, ktm, f"ktm{par}", 2 + par, True),
                                              (jv, vtm, f"vtm{par}", 4 + par, False)):
                    wsl = bfc(WST_B + jm * 8192, 128, 4096)
                    pk = f"ps{pb}"
                    def kvp(e, tc=tc, wsl=wsl, pb=pb):
                        last = None
                        for kc in range(16):
                            last = e.matmul(
                                ps[pb][:, 0:256],
                                xnc(kc)[:, tc * 128:tc * 128 + 128],
                                wsl[:, kc * 256:kc * 256 + 256],
                                start=(kc == 0), stop=(kc == 15),
                                skip_group_check=True)
                        return last
                    P.op("pe", kvp,
                         reads=[wkey(jm)] + [f"xn{kc}.{tc // 4}" for kc in range(16)],
                         writes=[pk])
                    if relu:
                        P.op("act", lambda e, dst=dst, pb=pb: e.activation(
                            dst, ps[pb][:, 0:256], AF.Relu),
                            reads=[pk], writes=[dk])
                    else:
                        P.op("act", lambda e, dst=dst, pb=pb: e.copy(
                            dst, ps[pb][:, 0:256]),
                            reads=[pk], writes=[dk])

                def kva(e, si=si, tc=tc, ktm=ktm, vtm=vtm):
                    last = None
                    for hh in range(4):
                        h = si * 4 + hh
                        ch, p_ = h // 2, h % 2
                        bank = 6 + (ch // 8)
                        last = e.matmul(
                            ps[bank][p_ * 64:p_ * 64 + 64,
                                     (ch % 8) * 64:(ch % 8) * 64 + 64],
                            ktm[:, hh * 64:hh * 64 + 64],
                            vtm[:, hh * 64:hh * 64 + 64],
                            start=(tc == 0), stop=(tc == 15),
                            skip_group_check=True)
                    return last
                wr = ["ps6", "ps7"] if (si == 0 and tc == 0) or (si == 7 and tc == 15) else []
                P.op("pe", kva, reads=[f"ktm{par}", f"vtm{par}"], writes=wr)

                def ksa(e, si=si, tc=tc, ktm=ktm):
                    last = None
                    for jc in range(2):
                        icg = si * 2 + jc
                        last = e.matmul(
                            ps[0][:, icg:icg + 1],
                            ktm[:, jc * 128:jc * 128 + 128], c_1b,
                            start=(tc == 0), stop=(tc == 15),
                            skip_group_check=True)
                    return last
                wr = ["ps0"] if (si == 0 and tc == 0) or (si == 7 and tc == 15) else []
                P.op("pe", ksa, reads=[f"ktm{par}", "c_1b"], writes=wr)
        ccs = f32c(CC_B, 128, 1040)
        cc2 = ccs
        P.op("dve", lambda e: e.tensor_copy(ccs[:, 0:512], ps[6][:, :]),
             reads=["ps6"], writes=["ccs", "w1b"])
        P.op("dve", lambda e: e.tensor_copy(ccs[:, 512:1024], ps[7][:, :]),
             reads=["ps7"], writes=["ccs"])
        P.op("dve", lambda e: e.tensor_copy(ccs[:, 1024:1040], ps[0][:, 0:16]),
             reads=["ps0"], writes=["ccs"])
        P.op("sp", lambda e: e.dma_start(out=cci.ap(), in_=ccs),
             reads=["ccs"], writes=["cci"], dma=True)
        P.op("pool", lambda e: e.collective_compute(
            "AllReduce", ALU.add,
            replica_groups=[[0, 1, 2, 3], [4, 5, 6, 7]],
            ins=[cci.ap().opt()], outs=[cco.ap().opt()]),
            reads=["cci"], writes=["cco"])
        P.op("sp", lambda e: e.dma_start(out=cc2, in_=cco.ap()),
             reads=["cco"], writes=["cc2", "ccs", "w1b"], dma=True)
        for si in range(8):
            jq = wload(si, wqr)
            wq_s = bfc(WST_B + jq * 8192, 128, 4096)
            for ic in range(2):
                icg = si * 2 + ic
                for t4 in range(4):
                    b = (ic * 4 + t4) % 4
                    def qmm(e, ic=ic, t4=t4, b=b, wq_s=wq_s):
                        last = None
                        for kc in range(16):
                            o = kc * 256 + ic * 128
                            last = e.matmul(
                                ps[b][:, :], wq_s[:, o:o + 128],
                                xnc(kc)[:, t4 * 512:t4 * 512 + 512],
                                start=(kc == 0), stop=(kc == 15),
                                skip_group_check=True)
                        return last
                    P.op("pe", qmm,
                         reads=[wkey(jq)] + [f"xn{kc}.{t4}" for kc in range(16)],
                         writes=[f"ps{b}"])
                    P.op("act", lambda e, icg=icg, t4=t4, b=b: e.activation(
                        qac(icg)[:, t4 * 512:t4 * 512 + 512], ps[b][:, :], AF.Relu),
                        reads=[f"ps{b}"], writes=[f"qa{icg}.{t4}"])
        kvb = bfc(KVB_B, 128, 1024)
        P.op("dve", lambda e: e.tensor_copy(kvb[:, :], cc2[:, 0:1024]),
             reads=["cc2", "w1b"], writes=["kvb"])
        selt_s = f32c(SELT_B, 128, 512)
        P.op("sp", lambda e: e.dma_start(out=selt_s, in_=seltd),
             writes=["selt"], dma=True)
        for ic in range(16):
            P.op("dve", lambda e, ic=ic: e.tensor_scalar(
                bfc(KSM_B + ic * 64, 128, 32),
                selt_s[:, ic * 32:ic * 32 + 32],
                cc2[:, 1024 + ic:1024 + ic + 1], None, ALU.mult),
                reads=["selt", "cc2", "w1b"], writes=[f"ksm{ic}"])

    def plan_attn_b(out_dram, okey):
        kvb = bfc(KVB_B, 128, 1024)
        recip = f32c(RECIP_B, 32, 2048)
        sel_s = f32c(SHR_B, 32, 2048)
        rb = [f32c(SHR_B + 8192, 128, 512), f32c(SHR_B + 10240, 128, 512)]
        P.op("sp", lambda e: e.dma_start(out=sel_s, in_=seld),
             writes=["sel"], dma=True)
        for t4 in range(4):
            b = t4 % 2
            def nmm(e, t4=t4, b=b):
                last = None
                for ic in range(16):
                    last = e.matmul(
                        ps[b][0:32, :], bfc(KSM_B + ic * 64, 128, 32),
                        qac(ic)[:, t4 * 512:t4 * 512 + 512],
                        start=(ic == 0), stop=(ic == 15), skip_group_check=True)
                return last
            P.op("pe", nmm,
                 reads=[f"ksm{ic}" for ic in range(16)] + ["w1b"] +
                       [f"qa{ic}.{t4}" for ic in range(16)],
                 writes=[f"ps{b}"])
            P.op("dve", lambda e, b=b: e.tensor_scalar_max(
                sg[0][0:32, :], ps[b][0:32, :], 1e-6),
                reads=[f"ps{b}"], writes=["sg0"])
            P.op("dve", lambda e, t4=t4: e.reciprocal(
                recip[:, t4 * 512:t4 * 512 + 512], sg[0][0:32, :]),
                reads=["sg0"],
                writes=[f"recip{t4}"] + (_xncov(RECIP_B, RECIP_B + 8192)
                                         if t4 == 0 else []))
        for ic in range(16):
            for t4 in range(4):
                brb = 2 + (t4 % 2)
                ba = 4 + (t4 % 2)
                P.op("pe", lambda e, ic=ic, t4=t4, brb=brb: e.matmul(
                    ps[brb][:, :], sel_s[:, ic * 128:ic * 128 + 128],
                    recip[:, t4 * 512:t4 * 512 + 512],
                    start=True, stop=True, skip_group_check=True),
                    reads=["sel", f"recip{t4}"], writes=[f"ps{brb}"])
                P.op("act", lambda e, t4=t4, brb=brb: e.copy(rb[t4 % 2], ps[brb][:, :]),
                     reads=[f"ps{brb}"],
                     writes=[f"rb{t4 % 2}"] +
                            (["ktm0", "ktm1", "vtm0", "vtm1"] if t4 % 2 == 0 else []))

                def amm(e, ic=ic, t4=t4, ba=ba):
                    qs = qac(ic)
                    e.matmul(ps[ba][0:64, :], kvb[0:64, ic * 64:ic * 64 + 64],
                             qs[0:64, t4 * 512:t4 * 512 + 512],
                             start=True, stop=True, skip_group_check=True)
                    return e.matmul(ps[ba][64:128, :],
                                    kvb[64:128, ic * 64:ic * 64 + 64],
                                    qs[64:128, t4 * 512:t4 * 512 + 512],
                                    start=True, stop=True, skip_group_check=True)
                P.op("pe", amm, reads=["kvb", "w1b", f"qa{ic}.{t4}"], writes=[f"ps{ba}"])
                P.op("dve", lambda e, ic=ic, t4=t4, ba=ba: e.tensor_tensor(
                    qac(ic)[:, t4 * 512:t4 * 512 + 512], ps[ba][:, :], rb[t4 % 2],
                    ALU.mult),
                    reads=[f"ps{ba}", f"rb{t4 % 2}"], writes=[f"qa{ic}.{t4}"])
        wor = wo.rearrange("(ic p) m -> p ic m", p=128)
        for msl in range(8):
            slot = msl % 2
            wslab = bfc(WST_B + slot * 8192, 128, 4096)
            wkeys = [f"w0{'ab'[slot]}"]
            P.op("pool", lambda e, msl=msl, wslab=wslab: e.dma_start(
                out=wslab.rearrange("p (ic m) -> p ic m", ic=16),
                in_=wor[:, :, msl * 256:msl * 256 + 256]),
                writes=wkeys, dma=True)
            for mc in range(2):
                mg = msl * 2 + mc
                for t4 in range(4):
                    b = 6 + (t4 % 2)
                    def womm(e, mc=mc, t4=t4, b=b, wslab=wslab):
                        last = None
                        for ic in range(16):
                            o = ic * 256 + mc * 128
                            last = e.matmul(
                                ps[b][:, :], wslab[:, o:o + 128],
                                qac(ic)[:, t4 * 512:t4 * 512 + 512],
                                start=(ic == 0), stop=(ic == 15),
                                skip_group_check=True)
                        return last
                    P.op("pe", womm,
                         reads=wkeys + [f"qa{ic}.{t4}" for ic in range(16)],
                         writes=[f"ps{b}"])
                    j = (mc * 4 + t4) % 2
                    xt = xf[:, j * 512:j * 512 + 512]
                    xk = [f"xf{2 * j}", f"xf{2 * j + 1}"]
                    P.op("act", lambda e, mg=mg, t4=t4, xt=xt: e.dma_start(
                        out=xt,
                        in_=x1d.ap()[mg * 128:mg * 128 + 128, t4 * 512:t4 * 512 + 512]),
                        reads=[f"x1.{mg}.{t4}"], writes=xk, dma=True)
                    P.op("act", lambda e, b=b, mg=mg, j=j: e.activation(
                        sg[j], ps[b][:, :], AF.Identity, bias=c_bo[:, mg:mg + 1]),
                        reads=[f"ps{b}", "c_bo"], writes=[f"sg{j}"])
                    P.op("dve", lambda e, j=j, xt=xt: e.tensor_tensor(
                        sg[2 + j], sg[j], xt, ALU.add),
                        reads=[f"sg{j}"] + xk, writes=[f"sg{2 + j}"])
                    P.op("sp", lambda e, mg=mg, t4=t4, j=j: e.dma_start(
                        out=out_dram[mg * 128:mg * 128 + 128,
                                     t4 * 512:t4 * 512 + 512],
                        in_=sg[2 + j]),
                        reads=[f"sg{2 + j}"], writes=[f"{okey}.{mg}.{t4}"], dma=True)

    nokeys = lambda *a: []
    p0 = make_p0_thunks()
    plan_ln(xin, nokeys, side=p0, nside=8)
    plan_ffn_up(p0)
    while p0:
        p0.pop(0)()
    if nphase == 1:
        for i in range(16):
            P.op("pool", lambda e, i=i: e.dma_start(
                out=out[i * 128:i * 128 + 128, :],
                in_=w2s.ap()[i * 128:i * 128 + 128, :]),
                reads=["w2s"], writes=[f"fo.{i}.0"], dma=True)
    elif nphase == 2:
        plan_ffn_down(xin, out, nokeys, "fo")
    else:
        plan_ffn_down(xin, x1d.ap(), nokeys, "x1")
        plan_ln(x1d.ap(), lambda t4: [f"x1.{mg}.{t4}" for mg in range(16)])
        plan_attn_a()
        if nphase == 3:
            for i in range(16):
                P.op("pool", lambda e, i=i: e.dma_start(
                    out=out[i * 128:i * 128 + 128, :], in_=qac(i)),
                    reads=[f"qa{i}.{t}" for t in range(4)],
                    writes=[f"fo.{i}.0"], dma=True)
        elif nphase == 4:
            plan_attn_b(out, "fo")
        else:
            plan_attn_b(x2d.ap(), "x2")
            plan_ln(x2d.ap(), lambda t4: [f"x2.{mg}.{t4}" for mg in range(16)])
            plan_ffn_up(None)
            plan_ffn_down(x2d.ap(),
                          out, lambda mg, t4: [f"x2.{mg}.{t4}"], "fo")

    final = dict(P.cnt)
    nc._plan_cnt = final
    with nc.Block() as block:
        @block.sync
        def _(eng):
            P.run("sp", eng)
            for s in sem_names:
                if final[s] > 0:
                    eng.wait_ge(sems[s], final[s])

        @block.tensor
        def _(eng):
            P.run("pe", eng)

        @block.scalar
        def _(eng):
            P.run("act", eng)

        @block.vector
        def _(eng):
            P.run("dve", eng)

        @block.gpsimd
        def _(eng):
            P.run("pool", eng)

    es.close()
    return nc


def kernel(L0_attn_hidden_states, L0_attn_Wq, L0_attn_Wk, L0_attn_Wv,
           L0_attn_Wo, L0_attn_bo, L0_ffn_W1, L0_ffn_W2, L0_ffn_W3):
    x = np.asarray(L0_attn_hidden_states, dtype=np.float32)
    w1t = np.ascontiguousarray(np.asarray(L0_ffn_W1, np.float32).T)
    w3t = np.ascontiguousarray(np.asarray(L0_ffn_W3, np.float32).T)
    w2t = np.ascontiguousarray(np.asarray(L0_ffn_W2, np.float32).T)
    wq = np.ascontiguousarray(np.asarray(L0_attn_Wq, np.float32))
    wk = np.ascontiguousarray(np.asarray(L0_attn_Wk, np.float32))
    wv = np.ascontiguousarray(np.asarray(L0_attn_Wv, np.float32))
    wo = np.ascontiguousarray(np.asarray(L0_attn_Wo, np.float32))
    bo = np.asarray(L0_attn_bo, np.float32)
    bo_t = np.ascontiguousarray(bo.reshape(16, 128).T)

    hs = np.ascontiguousarray((_had(128) * (0.5 / np.sqrt(8192.0))).astype(np.float32))
    h64 = np.ascontiguousarray(_had(64).astype(ml_dtypes.bfloat16))
    sel = np.zeros((32, 2048), np.float32)
    selt = np.zeros((128, 512), np.float32)
    for c in range(16):
        for j in range(128):
            h = 2 * c + (1 if j >= 64 else 0)
            sel[h, 128 * c + j] = 1.0
            selt[j, 32 * c + h] = 1.0

    nc = build_program(NPHASE)
    in_maps = []
    for g in range(8):
        b, q = g // 4, g % 4
        xs = np.ascontiguousarray(x[b, q * 2048:(q + 1) * 2048, :].T)
        in_maps.append({
            "xin": xs, "w1t": w1t, "w3t": w3t, "w2t": w2t,
            "wq": wq, "wkd": wk, "wv": wv, "wo": wo, "bo_t": bo_t,
            "hs128": hs, "h64d": h64, "seld": sel, "seltd": selt,
        })
    res = run_bass_kernel_spmd(nc, in_maps, core_ids=list(range(8)))
    outp = np.empty((2, 8192, 2048), np.float32)
    for g in range(8):
        b, q = g // 4, g % 4
        outp[b, q * 2048:(q + 1) * 2048, :] = res.results[g]["out"].T
    return outp


# version 17
# speedup vs baseline: 1.0012x; 1.0012x over previous
import numpy as np
import ml_dtypes
import concourse.bass as bass
import concourse.mybir as mybir
from concourse.bass_utils import run_bass_kernel_spmd
from contextlib import ExitStack

F32 = mybir.dt.float32
BF16 = mybir.dt.bfloat16
U8 = mybir.dt.uint8
AF = mybir.ActivationFunctionType
ALU = mybir.AluOpType

NPHASE = 6
STRICT = False

D = 2048
T = 2048
HID = 8192
LN_EPS = 1e-5

XN_B = 0
QA_B = 65536
WST_B = 131072
XF_B = 163840
SGEV_B = 180224
SHR_B = 188416
STRIP_B = 201728
CST_B = 205824
ARENA_B = 212480

CC_B = WST_B + 24576
KVB_B = WST_B + 28736
KSM_B = WST_B + 30784
RECIP_B = XN_B + 16512
SELT_B = CST_B + 2560


def _had(n):
    h = np.array([[1.0]])
    while h.shape[0] < n:
        h = np.block([[h, h], [h, -h]])
    return h


def _xncov(b0, b1):
    return [f"xn{c}.{t}" for c in range(b0 // 4096, (b1 - 1) // 4096 + 1)
            for t in range(4)]


def build_program(nphase):
    nc = bass.Bass("TRN2", target_bir_lowering=False)

    xin = nc.dram_tensor("xin", [D, T], F32, kind="ExternalInput").ap()
    w1t = nc.dram_tensor("w1t", [D, HID], F32, kind="ExternalInput").ap()
    w3t = nc.dram_tensor("w3t", [D, HID], F32, kind="ExternalInput").ap()
    w2t = nc.dram_tensor("w2t", [HID, D], F32, kind="ExternalInput").ap()
    wq = nc.dram_tensor("wq", [D, D], F32, kind="ExternalInput").ap()
    wkd = nc.dram_tensor("wkd", [D, D], F32, kind="ExternalInput").ap()
    wv = nc.dram_tensor("wv", [D, D], F32, kind="ExternalInput").ap()
    wo = nc.dram_tensor("wo", [D, D], F32, kind="ExternalInput").ap()
    bo_t = nc.dram_tensor("bo_t", [128, 16], F32, kind="ExternalInput").ap()
    hs128 = nc.dram_tensor("hs128", [128, 128], F32, kind="ExternalInput").ap()
    h64d = nc.dram_tensor("h64d", [64, 64], BF16, kind="ExternalInput").ap()
    seld = nc.dram_tensor("seld", [32, 2048], F32, kind="ExternalInput").ap()
    seltd = nc.dram_tensor("seltd", [128, 512], F32, kind="ExternalInput").ap()

    ydr = nc.dram_tensor("ydr", [HID, D], BF16)
    w2s = nc.dram_tensor("w2s", [HID, D], BF16)
    hidd = nc.dram_tensor("hidd", [HID, T], BF16)
    x1d = nc.dram_tensor("x1d", [D, T], F32)
    x2d = nc.dram_tensor("x2d", [D, T], F32)
    cci = nc.dram_tensor("cci", [128, 1040], F32)
    cco = nc.dram_tensor("cco", [128, 1040], F32)
    out = nc.dram_tensor("out", [D, T], F32, kind="ExternalOutput").ap()

    es = ExitStack()
    arena = es.enter_context(nc.sbuf_tensor("arena", [128, ARENA_B], U8))
    a_f32 = arena.bitcast(F32).ap()
    a_bf = arena.bitcast(BF16).ap()

    def f32c(boff, rows, cols):
        assert boff % 4 == 0 and boff + 4 * cols <= ARENA_B
        return a_f32[0:rows, boff // 4: boff // 4 + cols]

    def bfc(boff, rows, cols):
        assert boff % 2 == 0 and boff + 2 * cols <= ARENA_B
        return a_bf[0:rows, boff // 2: boff // 2 + cols]

    ps = [es.enter_context(nc.psum_tensor(f"ps{i}", [128, 512], F32)) for i in range(8)]
    NDS = 8
    sem_names = ["epe", "eact", "edve", "epool"]
    for q in ("sp", "act", "pool"):
        sem_names += [f"d{q}{j}" for j in range(NDS)]
    sems = {s: es.enter_context(nc.semaphore(s)) for s in sem_names}

    ESEM = {"pe": "epe", "act": "eact", "dve": "edve", "pool": "epool"}

    class Plan:
        def __init__(self):
            self.ops = {e: [] for e in ("sp", "pe", "act", "dve", "pool")}
            self.cnt = {s: 0 for s in sems}
            self.wm = {e: {} for e in self.ops}
            self.lastw = {}
            self.readers = {}
            self.dma_idx = {"sp": 0, "act": 0, "pool": 0}

        def op(self, eng, fn, reads=(), writes=(), dma=False):
            need = {}

            def add(sc):
                if sc is not None and need.get(sc[0], 0) < sc[1]:
                    need[sc[0]] = sc[1]

            if STRICT == "all" or STRICT is True:
                for s, c in self.cnt.items():
                    if c > 0:
                        add((s, c))
            elif STRICT == "dma" and dma:
                for s, c in self.cnt.items():
                    if s[0] == "d" and c > 0:
                        add((s, c))
            elif STRICT == "compute" and not dma:
                for s in ("epe", "eact", "edve", "epool"):
                    if self.cnt[s] > 0:
                        add((s, self.cnt[s]))
            for k in reads:
                add(self.lastw.get(k))
            for k in writes:
                add(self.lastw.get(k))
                for item in (self.readers.get(k) or {}).items():
                    add(item)
            if dma:
                j = self.dma_idx[eng] % NDS
                self.dma_idx[eng] += 1
                sem = f"d{eng}{j}"
                if self.cnt[sem] > 0:
                    add((sem, self.cnt[sem]))
                self.cnt[sem] += 16
            else:
                sem = ESEM[eng]
                self.cnt[sem] += 1
            post = (sem, self.cnt[sem])
            waits = []
            for s, c in sorted(need.items()):
                if eng == "pe" and s == "epe":
                    continue
                if c <= self.wm[eng].get(s, 0):
                    continue
                waits.append((s, c))
                self.wm[eng][s] = c
            for k in writes:
                self.lastw[k] = post
                self.readers[k] = {}
            for k in reads:
                d = self.readers.setdefault(k, {})
                if d.get(post[0], 0) < post[1]:
                    d[post[0]] = post[1]
            self.ops[eng].append((waits, fn, post))

        def run(self, ename, eng):
            for waits, fn, post in self.ops[ename]:
                for s, c in waits:
                    eng.wait_ge(sems[s], c)
                inst = fn(eng)
                inst.then_inc(sems[post[0]], 16 if post[0][0] == "d" else 1)

    P = Plan()

    c_hs = f32c(CST_B, 128, 128)
    c_h64 = bfc(CST_B + 512, 64, 64)
    c_h64b = a_bf[64:128, (CST_B + 640) // 2:(CST_B + 640) // 2 + 64]
    c_1f = f32c(CST_B + 768, 128, 1)
    c_1b = bfc(CST_B + 772, 128, 1)
    c_1r = f32c(CST_B + 784, 1, 128)
    c_bo = f32c(CST_B + 1296, 128, 16)
    c_xsq = [bfc(CST_B + 1360, 128, 256), bfc(CST_B + 1872, 128, 256)]

    P.op("sp", lambda e: e.dma_start(out=c_hs, in_=hs128), writes=["c_hs"], dma=True)
    P.op("sp", lambda e: e.dma_start(out=c_h64, in_=h64d), writes=["c_h64"], dma=True)
    P.op("sp", lambda e: e.dma_start(out=c_h64b, in_=h64d), writes=["c_h64b"], dma=True)
    P.op("sp", lambda e: e.dma_start(out=c_bo, in_=bo_t), writes=["c_bo"], dma=True)
    P.op("pool", lambda e: e.memset(c_1f, 1.0), writes=["c_1f"])
    P.op("pool", lambda e: e.memset(c_1b, 1.0), writes=["c_1b"])
    P.op("pool", lambda e: e.memset(c_1r, 1.0), writes=["c_1r"])

    xfb = bfc(XF_B, 128, 4096)
    xfb3 = xfb.rearrange("p (c t) -> p c t", c=16)
    xlb = bfc(QA_B + 57344, 128, 4096)
    xlb3 = xlb.rearrange("p (c t) -> p c t", c=16)
    xf = f32c(XF_B, 128, 4096)
    sg = [f32c(SGEV_B + j * 2048, 128, 512) for j in range(4)]

    def strip(j):
        return f32c(STRIP_B + j * 1024, 1, 256)

    def xnc(c):
        return bfc(XN_B + c * 4096, 128, 2048)

    def qac(i):
        return bfc(QA_B + i * 4096, 128, 2048)

    def plan_ln(src_dram, src_keys, side=None, nside=0, tiles=range(8)):
        src_r = src_dram.rearrange("(c p) t -> p c t", p=128)
        for tt in tiles:
            for _ in range(nside):
                if side:
                    side.pop(0)()
            t0 = tt * 256
            t4 = tt // 2
            x3 = xfb3 if tt % 2 == 0 else xlb3
            xk = ((lambda c: f"xf{c // 2}") if tt % 2 == 0
                  else (lambda c: f"qa{14 + c // 8}.{(c // 2) % 4}"))
            XALL = sorted({xk(c) for c in range(16)})
            if side is not None:
                bs = [0, 1] if tt % 2 == 0 else [2, 3]
                bb = [2, 3] if tt % 2 == 0 else [0, 1]
            else:
                bs = [0, 1] if tt % 2 == 0 else [2, 3]
                bb = [4, 5] if tt % 2 == 0 else [0, 1]
            P.op("pool", lambda e, t0=t0, x3=x3: e.dma_start(
                out=x3, in_=src_r[:, :, t0:t0 + 256]),
                reads=src_keys(t4), writes=XALL, dma=True)
            def sumx(e, x3=x3, bs=bs):
                last = None
                for c in range(16):
                    last = e.matmul(ps[bs[0]][0:1, 0:256], c_1b, x3[:, c, :],
                                    start=(c == 0), stop=(c == 15),
                                    skip_group_check=True)
                return last
            P.op("pe", sumx, reads=XALL + ["c_1b"], writes=[f"ps{bs[0]}"])
            for c in range(16):
                xq = c_xsq[c % 2]
                xqk = f"xsq{c % 2}"
                P.op("act", lambda e, c=c, xq=xq, x3=x3: e.activation(
                    xq, x3[:, c, :], AF.Square),
                    reads=[xk(c)], writes=[xqk])
                P.op("pe", lambda e, c=c, xq=xq, bs=bs: e.matmul(
                    ps[bs[1]][0:1, 0:256], c_1b, xq,
                    start=(c == 0), stop=(c == 15), skip_group_check=True),
                    reads=[xqk, "c_1b"],
                    writes=[f"ps{bs[1]}"] if c in (0, 15) else [])
            P.op("dve", lambda e, bs=bs: e.tensor_scalar_mul(
                strip(0), ps[bs[0]][0:1, 0:256], 1.0 / 2048.0),
                reads=[f"ps{bs[0]}"], writes=["st0"])
            P.op("dve", lambda e, bs=bs: e.tensor_scalar_mul(
                strip(1), ps[bs[1]][0:1, 0:256], 1.0 / 2048.0),
                reads=[f"ps{bs[1]}"], writes=["st1"])
            P.op("dve", lambda e: e.tensor_tensor(strip(2), strip(0), strip(0), ALU.mult),
                 reads=["st0"], writes=["st2"])
            P.op("dve", lambda e: e.tensor_sub(strip(1), strip(1), strip(2)),
                 reads=["st1", "st2"], writes=["st1"])
            P.op("dve", lambda e: e.tensor_scalar_add(strip(1), strip(1), LN_EPS),
                 reads=["st1"], writes=["st1"])
            P.op("act", lambda e: e.activation(strip(2), strip(1), AF.Sqrt),
                 reads=["st1"], writes=["st2"])
            P.op("dve", lambda e: e.reciprocal(strip(3), strip(2)),
                 reads=["st2"], writes=["st3"])
            P.op("dve", lambda e: e.tensor_tensor(strip(1), strip(0), strip(3), ALU.mult),
                 reads=["st0", "st3"], writes=["st1"])
            P.op("pe", lambda e, bb=bb: e.matmul(ps[bb[0]][:, 0:256], c_1r, strip(3),
                                                 start=True, stop=True,
                                                 skip_group_check=True),
                 reads=["st3", "c_1r"], writes=[f"ps{bb[0]}"])
            P.op("pe", lambda e, bb=bb: e.matmul(ps[bb[1]][:, 0:256], c_1r, strip(1),
                                                 start=True, stop=True,
                                                 skip_group_check=True),
                 reads=["st1", "c_1r"], writes=[f"ps{bb[1]}"])
            for c in range(16):
                sgj = c % 2
                P.op("dve", lambda e, c=c, sgj=sgj, x3=x3, bb=bb: e.tensor_tensor(
                    sg[sgj][:, 0:256], x3[:, c, :], ps[bb[0]][:, 0:256], ALU.mult),
                    reads=[xk(c), f"ps{bb[0]}"], writes=[f"sg{sgj}"])
                P.op("dve", lambda e, c=c, sgj=sgj, t0=t0, bb=bb: e.tensor_tensor(
                    xnc(c)[:, t0:t0 + 256], sg[sgj][:, 0:256], ps[bb[1]][:, 0:256],
                    ALU.subtract),
                    reads=[f"sg{sgj}", f"ps{bb[1]}"], writes=[f"xn{c}.{t4}"])

    def make_p0_thunks():
        NRING = 20
        DEPTH = 16

        def ring(j):
            return bfc(QA_B + 16384 + j * 2048, 128, 1024)

        def ringk(j):
            return [f"p0r{j}"]

        p0outs = [bfc(XF_B + 8192 + k * 2048, 128, 1024) for k in range(4)]
        p0oks = [[f"xf{8 + 2 * k}", f"xf{9 + 2 * k}"] for k in range(4)]

        def ringf(j):
            return f32c(QA_B + 16384 + j * 4096, 128, 1024)

        loads1, comps1, loads2, comps2 = [], [], [], []
        for cp in range(64):
            for mh in range(2):
                i = cp * 2 + mh
                s = i % (NRING // 2)
                ins = ringf(s)
                ik = [f"p0r{2 * s}", f"p0r{2 * s + 1}"]
                outs = p0outs[i % 4]
                ok = p0oks[i % 4]

                def ld(cp=cp, mh=mh, ins=ins, ik=ik):
                    P.op("act", lambda e: e.dma_start(
                        out=ins,
                        in_=w2t[cp * 128:cp * 128 + 128, mh * 1024:mh * 1024 + 1024]),
                        writes=ik, dma=True)

                def cm(cp=cp, mh=mh, i=i, ins=ins, outs=outs, ik=ik, ok=ok):
                    for q in range(2):
                        b = 4 + 2 * (i % 2) + q
                        P.op("pe", lambda e, q=q, b=b: e.matmul(
                            ps[b][:, :], c_hs, ins[:, q * 512:q * 512 + 512],
                            start=True, stop=True, skip_group_check=True),
                            reads=ik + ["c_hs"], writes=[f"ps{b}"])
                        P.op("dve", lambda e, q=q, b=b: e.tensor_copy(
                            outs[:, q * 512:q * 512 + 512], ps[b][:, :]),
                            reads=[f"ps{b}"], writes=ok)
                    P.op("sp", lambda e: e.dma_start(
                        out=ydr.ap()[cp * 128:cp * 128 + 128,
                                     mh * 1024:mh * 1024 + 1024],
                        in_=outs),
                        reads=ok, writes=[f"ydr{cp}"], dma=True)
                loads1.append(ld)
                comps1.append(cm)
        y_r2 = ydr.ap().rearrange("(c p) m -> p c m", p=128)
        ws_r2 = w2s.ap().rearrange("(c p) m -> p c m", p=128)
        first = [True]
        for dp in range(64):
            for mh in range(2):
                i = dp * 2 + mh
                s = i % NRING
                ins = ring(s)
                ik = ringk(s)
                outs = p0outs[i % 4]
                ok = p0oks[i % 4]

                def ld(dp=dp, mh=mh, ins=ins, ik=ik):
                    rd = ik + ([f"ydr{cp}" for cp in range(64)]
                               if first[0] else [f"ydr{0}"])
                    first[0] = False
                    P.op("sp", lambda e: e.dma_start(
                        out=ins,
                        in_=y_r2[2 * dp:2 * dp + 2, :, mh * 1024:mh * 1024 + 1024]),
                        reads=rd, writes=ik, dma=True)

                def cm(dp=dp, mh=mh, i=i, ins=ins, outs=outs, ik=ik, ok=ok):
                    for q in range(2):
                        b = 4 + 2 * (i % 2) + q
                        def mmp(e, q=q, b=b, ins=ins):
                            e.matmul(
                                ps[b][0:64, :], c_h64,
                                ins[0:64, q * 512:q * 512 + 512],
                                start=True, stop=True, skip_group_check=True)
                            return e.matmul(
                                ps[b][64:128, :], c_h64b,
                                ins[64:128, q * 512:q * 512 + 512],
                                start=True, stop=True, skip_group_check=True)
                        P.op("pe", mmp, reads=ik + ["c_h64", "c_h64b"],
                             writes=[f"ps{b}"])
                        P.op("dve", lambda e, q=q, b=b: e.tensor_copy(
                            outs[:, q * 512:q * 512 + 512], ps[b][:, :]),
                            reads=[f"ps{b}"], writes=ok)
                    P.op("sp", lambda e: e.dma_start(
                        out=ws_r2[2 * dp:2 * dp + 2, :, mh * 1024:mh * 1024 + 1024],
                        in_=outs),
                        reads=ok, writes=["w2s"], dma=True)
                loads2.append(ld)
                comps2.append(cm)

        thunks = []
        for loads, comps, dep in ((loads1, comps1, 8), (loads2, comps2, DEPTH)):
            n = len(loads)
            for k in range(n + dep):
                def th(k=k, loads=loads, comps=comps, n=n, dep=dep):
                    if k < n:
                        loads[k]()
                    if k >= dep:
                        comps[k - dep]()
                thunks.append(th)
        return thunks

    def plan_ffn_up(side):
        w1r = w1t.rearrange("(kc p) h -> p kc h", p=128)
        w3r = w3t.rearrange("(kc p) h -> p kc h", p=128)
        pos = [0]
        pend = []
        for hs in range(32):
            while pend:
                pend.pop(0)()
            for _ in range(7):
                if side:
                    side.pop(0)()
            slot = hs % 2
            wslab = bfc(WST_B + slot * 16384, 128, 8192)
            ka, kb = f"w{slot}a", f"w{slot}b"
            P.op("pool", lambda e, hs=hs, wslab=wslab: e.dma_start(
                out=wslab[:, 0:4096].rearrange("p (kc h) -> p kc h", kc=16),
                in_=w1r[:, :, hs * 256:hs * 256 + 256]),
                writes=[ka], dma=True)
            P.op("pool", lambda e, hs=hs, wslab=wslab: e.dma_start(
                out=wslab[:, 4096:8192].rearrange("p (kc h) -> p kc h", kc=16),
                in_=w3r[:, :, hs * 256:hs * 256 + 256]),
                writes=[kb], dma=True)
            W = 2 if side is not None else 4
            npos = (4 if side is not None else 8) // W
            for hc in range(2):
                hcg = hs * 2 + hc
                qslot = hcg % 4
                slab = qac(qslot)
                for w0 in range(0, 4, W):
                    gb = {}
                    for off, wkk, mat in ((0, ka, "g"), (4096, kb, "l")):
                        p = pos[0]
                        pos[0] = (pos[0] + 1) % npos
                        banks = [p * W + j for j in range(W)]
                        gb[mat] = banks

                        def mmg(e, off=off, banks=banks, hc=hc, w0=w0, wslab=wslab):
                            last = None
                            for kc in range(16):
                                o = off + kc * 256 + hc * 128
                                for j, b in enumerate(banks):
                                    last = e.matmul(
                                        ps[b][:, :], wslab[:, o:o + 128],
                                        xnc(kc)[:, (w0 + j) * 512:(w0 + j) * 512 + 512],
                                        start=(kc == 0), stop=(kc == 15),
                                        skip_group_check=True)
                            return last
                        P.op("pe", mmg,
                             reads=[wkk] + [f"xn{kc}.{w0 + j}" for kc in range(16)
                                            for j in range(W)],
                             writes=[f"ps{b}" for b in banks])
                    for j in range(W):
                        t4 = w0 + j
                        bg, bl = gb["g"][j], gb["l"][j]
                        sgj = t4 % 4
                        P.op("act", lambda e, bg=bg, sgj=sgj: e.activation(
                            sg[sgj], ps[bg][:, :], AF.Silu),
                            reads=[f"ps{bg}"], writes=[f"sg{sgj}"])
                        P.op("dve", lambda e, bl=bl, sgj=sgj, slab=slab, t4=t4,
                             qslot=qslot: e.tensor_tensor(
                            slab[:, t4 * 512:t4 * 512 + 512], sg[sgj], ps[bl][:, :],
                            ALU.mult),
                            reads=[f"sg{sgj}", f"ps{bl}"], writes=[f"qa{qslot}.{t4}"])
                def st(hcg=hcg, slab=slab, qslot=qslot):
                    P.op("sp", lambda e: e.dma_start(
                        out=hidd.ap()[hcg * 128:hcg * 128 + 128, :], in_=slab),
                        reads=[f"qa{qslot}.{t}" for t in range(4)],
                        writes=[f"hid{hcg}"], dma=True)
                pend.append(st)
        while pend:
            pend.pop(0)()

    def plan_ffn_down(resid_dram, out_dram, resid_keys, okey, mid=None):
        hid_r = hidd.ap().rearrange("(hc p) t -> p hc t", p=128)
        ws_r2 = w2s.ap().rearrange("(hc p) m -> p hc m", p=128)

        def cover(hclo, hchi):
            keys = []
            for hc in range(hclo, hchi):
                if hc < 32:
                    keys += [f"xn{hc // 2}.{t}" for t in range(4)]
                else:
                    keys += [f"qa{(hc - 32) // 2}.{t}" for t in range(4)]
            return sorted(set(keys))

        for th in range(2):
            if th == 1 and mid is not None:
                mid()
            for g in range(8):
                dst = a_bf[:, g * 8192: g * 8192 + 8192]
                P.op("sp", lambda e, th=th, g=g, dst=dst: e.dma_start(
                    out=dst.rearrange("p (hc t) -> p hc t", hc=8),
                    in_=hid_r[:, g * 8:g * 8 + 8, th * 1024:th * 1024 + 1024]),
                    reads=[f"hid{hc}" for hc in range(g * 8, g * 8 + 8)] + ["w2s"],
                    writes=cover(g * 8, g * 8 + 8), dma=True)
            for ms in range(8):
                banks = [4 * (ms % 2) + j for j in range(4)]
                for sub in range(2):
                    wsl = (ms * 2 + sub) % 2
                    wslab = bfc(WST_B + wsl * 16384, 128, 8192)
                    wkeys = [f"w{wsl}a", f"w{wsl}b"]
                    rd = ["w2s"] if (th == 0 and ms == 0 and sub == 0) else []
                    P.op("act", lambda e, ms=ms, sub=sub, wslab=wslab: e.dma_start(
                        out=wslab.rearrange("p (hc m) -> p hc m", hc=32),
                        in_=ws_r2[:, sub * 32:sub * 32 + 32, ms * 256:ms * 256 + 256]),
                        reads=rd, writes=wkeys, dma=True)

                    for part in range(4):
                        def mmh(e, sub=sub, part=part, wslab=wslab, banks=banks):
                            last = None
                            for hcl in range(part * 8, part * 8 + 8):
                                hc = sub * 32 + hcl
                                hbuf = a_bf[:, hc * 1024: hc * 1024 + 1024]
                                for mc in range(2):
                                    for t2 in range(2):
                                        last = e.matmul(
                                            ps[banks[mc * 2 + t2]][:, :],
                                            wslab[:, hcl * 256 + mc * 128:
                                                  hcl * 256 + mc * 128 + 128],
                                            hbuf[:, t2 * 512:t2 * 512 + 512],
                                            start=(hc == 0), stop=(hc == 63),
                                            skip_group_check=True)
                            return last
                        P.op("pe", mmh,
                             reads=wkeys + cover(sub * 32 + part * 8,
                                                 sub * 32 + part * 8 + 8),
                             writes=[f"ps{b}" for b in banks] if part in (0, 3)
                                    else [])
                for mc in range(2):
                    for t2 in range(2):
                        j = mc * 2 + t2
                        b = banks[j]
                        mg = ms * 2 + mc
                        t4 = th * 2 + t2
                        r0, c0 = mg * 128, t4 * 512
                        xt = xf[:, j * 512:j * 512 + 512]
                        xk = [f"xf{2 * j}", f"xf{2 * j + 1}"]
                        P.op("act", lambda e, r0=r0, c0=c0, xt=xt, mg=mg, t4=t4:
                             e.dma_start(out=xt,
                                         in_=resid_dram[r0:r0 + 128, c0:c0 + 512]),
                             reads=resid_keys(mg, t4), writes=xk, dma=True)
                        P.op("dve", lambda e, b=b, xt=xt, j=j: e.tensor_tensor(
                            sg[j], ps[b][:, :], xt, ALU.add),
                            reads=[f"ps{b}"] + xk, writes=[f"sg{j}"])
                        P.op("sp", lambda e, r0=r0, c0=c0, j=j: e.dma_start(
                            out=out_dram[r0:r0 + 128, c0:c0 + 512], in_=sg[j]),
                            reads=[f"sg{j}"], writes=[f"{okey}.{mg}.{t4}"], dma=True)

    def plan_attn_a():
        wqr = wq.rearrange("(kc p) i -> p kc i", p=128)
        wkr = wkd.rearrange("(kc p) i -> p kc i", p=128)
        wvr = wv.rearrange("(kc p) i -> p kc i", p=128)
        ktm_s = [bfc(SHR_B + 8192, 128, 256), bfc(SHR_B + 8704, 128, 256)]
        vtm_s = [bfc(SHR_B + 9216, 128, 256), bfc(SHR_B + 9728, 128, 256)]
        wcnt = [0]

        def wkey(j):
            return f"w{j // 2}{'ab'[j % 2]}"

        def wload(si, src):
            j = wcnt[0] % 3
            wcnt[0] += 1
            wsl = bfc(WST_B + j * 8192, 128, 4096)
            P.op("pool", lambda e, si=si, src=src, wsl=wsl: e.dma_start(
                out=wsl.rearrange("p (kc i) -> p kc i", kc=16),
                in_=src[:, :, si * 256:si * 256 + 256]),
                writes=[wkey(j)], dma=True)
            return j

        for si in range(8):
            jk = wload(si, wkr)
            jv = wload(si, wvr)
            for tc in range(16):
                par = tc % 2
                ktm, vtm = ktm_s[par], vtm_s[par]
                for jm, dst, dk, pb, relu in ((jk, ktm, f"ktm{par}", 2 + par, True),
                                              (jv, vtm, f"vtm{par}", 4 + par, False)):
                    wsl = bfc(WST_B + jm * 8192, 128, 4096)
                    pk = f"ps{pb}"
                    def kvp(e, tc=tc, wsl=wsl, pb=pb):
                        last = None
                        for kc in range(16):
                            last = e.matmul(
                                ps[pb][:, 0:256],
                                xnc(kc)[:, tc * 128:tc * 128 + 128],
                                wsl[:, kc * 256:kc * 256 + 256],
                                start=(kc == 0), stop=(kc == 15),
                                skip_group_check=True)
                        return last
                    P.op("pe", kvp,
                         reads=[wkey(jm)] + [f"xn{kc}.{tc // 4}" for kc in range(16)],
                         writes=[pk])
                    if relu:
                        P.op("act", lambda e, dst=dst, pb=pb: e.activation(
                            dst, ps[pb][:, 0:256], AF.Relu),
                            reads=[pk], writes=[dk])
                    else:
                        P.op("act", lambda e, dst=dst, pb=pb: e.copy(
                            dst, ps[pb][:, 0:256]),
                            reads=[pk], writes=[dk])

                def kva(e, si=si, tc=tc, ktm=ktm, vtm=vtm):
                    last = None
                    for hh in range(4):
                        h = si * 4 + hh
                        ch, p_ = h // 2, h % 2
                        bank = 6 + (ch // 8)
                        last = e.matmul(
                            ps[bank][p_ * 64:p_ * 64 + 64,
                                     (ch % 8) * 64:(ch % 8) * 64 + 64],
                            ktm[:, hh * 64:hh * 64 + 64],
                            vtm[:, hh * 64:hh * 64 + 64],
                            start=(tc == 0), stop=(tc == 15),
                            skip_group_check=True)
                    return last
                wr = ["ps6", "ps7"] if (si == 0 and tc == 0) or (si == 7 and tc == 15) else []
                P.op("pe", kva, reads=[f"ktm{par}", f"vtm{par}"], writes=wr)

                def ksa(e, si=si, tc=tc, ktm=ktm):
                    last = None
                    for jc in range(2):
                        icg = si * 2 + jc
                        last = e.matmul(
                            ps[0][:, icg:icg + 1],
                            ktm[:, jc * 128:jc * 128 + 128], c_1b,
                            start=(tc == 0), stop=(tc == 15),
                            skip_group_check=True)
                    return last
                wr = ["ps0"] if (si == 0 and tc == 0) or (si == 7 and tc == 15) else []
                P.op("pe", ksa, reads=[f"ktm{par}", "c_1b"], writes=wr)
        ccs = f32c(CC_B, 128, 1040)
        cc2 = ccs
        P.op("dve", lambda e: e.tensor_copy(ccs[:, 0:512], ps[6][:, :]),
             reads=["ps6"], writes=["ccs", "w1b"])
        P.op("dve", lambda e: e.tensor_copy(ccs[:, 512:1024], ps[7][:, :]),
             reads=["ps7"], writes=["ccs"])
        P.op("dve", lambda e: e.tensor_copy(ccs[:, 1024:1040], ps[0][:, 0:16]),
             reads=["ps0"], writes=["ccs"])
        P.op("sp", lambda e: e.dma_start(out=cci.ap(), in_=ccs),
             reads=["ccs"], writes=["cci"], dma=True)
        P.op("pool", lambda e: e.collective_compute(
            "AllReduce", ALU.add,
            replica_groups=[[0, 1, 2, 3], [4, 5, 6, 7]],
            ins=[cci.ap().opt()], outs=[cco.ap().opt()]),
            reads=["cci"], writes=["cco"])
        P.op("sp", lambda e: e.dma_start(out=cc2, in_=cco.ap()),
             reads=["cco"], writes=["cc2", "ccs", "w1b"], dma=True)
        for si in range(8):
            jq = wload(si, wqr)
            wq_s = bfc(WST_B + jq * 8192, 128, 4096)
            for ic in range(2):
                icg = si * 2 + ic
                for t4 in range(4):
                    b = (ic * 4 + t4) % 4
                    def qmm(e, ic=ic, t4=t4, b=b, wq_s=wq_s):
                        last = None
                        for kc in range(16):
                            o = kc * 256 + ic * 128
                            last = e.matmul(
                                ps[b][:, :], wq_s[:, o:o + 128],
                                xnc(kc)[:, t4 * 512:t4 * 512 + 512],
                                start=(kc == 0), stop=(kc == 15),
                                skip_group_check=True)
                        return last
                    P.op("pe", qmm,
                         reads=[wkey(jq)] + [f"xn{kc}.{t4}" for kc in range(16)],
                         writes=[f"ps{b}"])
                    P.op("act", lambda e, icg=icg, t4=t4, b=b: e.activation(
                        qac(icg)[:, t4 * 512:t4 * 512 + 512], ps[b][:, :], AF.Relu),
                        reads=[f"ps{b}"], writes=[f"qa{icg}.{t4}"])
        kvb = bfc(KVB_B, 128, 1024)
        P.op("dve", lambda e: e.tensor_copy(kvb[:, :], cc2[:, 0:1024]),
             reads=["cc2", "w1b"], writes=["kvb"])
        selt_s = f32c(SELT_B, 128, 512)
        P.op("sp", lambda e: e.dma_start(out=selt_s, in_=seltd),
             writes=["selt"], dma=True)
        for ic in range(16):
            P.op("dve", lambda e, ic=ic: e.tensor_scalar(
                bfc(KSM_B + ic * 64, 128, 32),
                selt_s[:, ic * 32:ic * 32 + 32],
                cc2[:, 1024 + ic:1024 + ic + 1], None, ALU.mult),
                reads=["selt", "cc2", "w1b"], writes=[f"ksm{ic}"])

    def plan_attn_b(out_dram, okey):
        kvb = bfc(KVB_B, 128, 1024)
        recip = f32c(RECIP_B, 32, 2048)
        sel_s = f32c(SHR_B, 32, 2048)
        rb = [f32c(SHR_B + 8192, 128, 512), f32c(SHR_B + 10240, 128, 512)]
        P.op("sp", lambda e: e.dma_start(out=sel_s, in_=seld),
             writes=["sel"], dma=True)
        for t4 in range(4):
            b = t4 % 2
            def nmm(e, t4=t4, b=b):
                last = None
                for ic in range(16):
                    last = e.matmul(
                        ps[b][0:32, :], bfc(KSM_B + ic * 64, 128, 32),
                        qac(ic)[:, t4 * 512:t4 * 512 + 512],
                        start=(ic == 0), stop=(ic == 15), skip_group_check=True)
                return last
            P.op("pe", nmm,
                 reads=[f"ksm{ic}" for ic in range(16)] + ["w1b"] +
                       [f"qa{ic}.{t4}" for ic in range(16)],
                 writes=[f"ps{b}"])
            P.op("dve", lambda e, b=b: e.tensor_scalar_max(
                sg[0][0:32, :], ps[b][0:32, :], 1e-6),
                reads=[f"ps{b}"], writes=["sg0"])
            P.op("dve", lambda e, t4=t4: e.reciprocal(
                recip[:, t4 * 512:t4 * 512 + 512], sg[0][0:32, :]),
                reads=["sg0"],
                writes=[f"recip{t4}"] + (_xncov(RECIP_B, RECIP_B + 8192)
                                         if t4 == 0 else []))
        for ic in range(16):
            for t4 in range(4):
                brb = 2 + (t4 % 2)
                ba = 4 + (t4 % 2)
                P.op("pe", lambda e, ic=ic, t4=t4, brb=brb: e.matmul(
                    ps[brb][:, :], sel_s[:, ic * 128:ic * 128 + 128],
                    recip[:, t4 * 512:t4 * 512 + 512],
                    start=True, stop=True, skip_group_check=True),
                    reads=["sel", f"recip{t4}"], writes=[f"ps{brb}"])
                P.op("act", lambda e, t4=t4, brb=brb: e.copy(rb[t4 % 2], ps[brb][:, :]),
                     reads=[f"ps{brb}"],
                     writes=[f"rb{t4 % 2}"] +
                            (["ktm0", "ktm1", "vtm0", "vtm1"] if t4 % 2 == 0 else []))

                def amm(e, ic=ic, t4=t4, ba=ba):
                    qs = qac(ic)
                    e.matmul(ps[ba][0:64, :], kvb[0:64, ic * 64:ic * 64 + 64],
                             qs[0:64, t4 * 512:t4 * 512 + 512],
                             start=True, stop=True, skip_group_check=True)
                    return e.matmul(ps[ba][64:128, :],
                                    kvb[64:128, ic * 64:ic * 64 + 64],
                                    qs[64:128, t4 * 512:t4 * 512 + 512],
                                    start=True, stop=True, skip_group_check=True)
                P.op("pe", amm, reads=["kvb", "w1b", f"qa{ic}.{t4}"], writes=[f"ps{ba}"])
                P.op("dve", lambda e, ic=ic, t4=t4, ba=ba: e.tensor_tensor(
                    qac(ic)[:, t4 * 512:t4 * 512 + 512], ps[ba][:, :], rb[t4 % 2],
                    ALU.mult),
                    reads=[f"ps{ba}", f"rb{t4 % 2}"], writes=[f"qa{ic}.{t4}"])
        wor = wo.rearrange("(ic p) m -> p ic m", p=128)
        for msl in range(8):
            slot = msl % 2
            wslab = bfc(WST_B + slot * 8192, 128, 4096)
            wkeys = [f"w0{'ab'[slot]}"]
            P.op("pool", lambda e, msl=msl, wslab=wslab: e.dma_start(
                out=wslab.rearrange("p (ic m) -> p ic m", ic=16),
                in_=wor[:, :, msl * 256:msl * 256 + 256]),
                writes=wkeys, dma=True)
            for mc in range(2):
                mg = msl * 2 + mc
                for t4 in range(4):
                    b = 6 + (t4 % 2)
                    def womm(e, mc=mc, t4=t4, b=b, wslab=wslab):
                        last = None
                        for ic in range(16):
                            o = ic * 256 + mc * 128
                            last = e.matmul(
                                ps[b][:, :], wslab[:, o:o + 128],
                                qac(ic)[:, t4 * 512:t4 * 512 + 512],
                                start=(ic == 0), stop=(ic == 15),
                                skip_group_check=True)
                        return last
                    P.op("pe", womm,
                         reads=wkeys + [f"qa{ic}.{t4}" for ic in range(16)],
                         writes=[f"ps{b}"])
                    j = (mc * 4 + t4) % 2
                    xt = xf[:, j * 512:j * 512 + 512]
                    xk = [f"xf{2 * j}", f"xf{2 * j + 1}"]
                    P.op("act", lambda e, mg=mg, t4=t4, xt=xt: e.dma_start(
                        out=xt,
                        in_=x1d.ap()[mg * 128:mg * 128 + 128, t4 * 512:t4 * 512 + 512]),
                        reads=[f"x1.{mg}.{t4}"], writes=xk, dma=True)
                    P.op("act", lambda e, b=b, mg=mg, j=j: e.activation(
                        sg[j], ps[b][:, :], AF.Identity, bias=c_bo[:, mg:mg + 1]),
                        reads=[f"ps{b}", "c_bo"], writes=[f"sg{j}"])
                    P.op("dve", lambda e, j=j, xt=xt: e.tensor_tensor(
                        sg[2 + j], sg[j], xt, ALU.add),
                        reads=[f"sg{j}"] + xk, writes=[f"sg{2 + j}"])
                    P.op("sp", lambda e, mg=mg, t4=t4, j=j: e.dma_start(
                        out=out_dram[mg * 128:mg * 128 + 128,
                                     t4 * 512:t4 * 512 + 512],
                        in_=sg[2 + j]),
                        reads=[f"sg{2 + j}"], writes=[f"{okey}.{mg}.{t4}"], dma=True)

    nokeys = lambda *a: []
    p0 = make_p0_thunks()
    plan_ln(xin, nokeys, side=p0, nside=8)
    plan_ffn_up(p0)
    while p0:
        p0.pop(0)()
    if nphase == 1:
        for i in range(16):
            P.op("pool", lambda e, i=i: e.dma_start(
                out=out[i * 128:i * 128 + 128, :],
                in_=w2s.ap()[i * 128:i * 128 + 128, :]),
                reads=["w2s"], writes=[f"fo.{i}.0"], dma=True)
    elif nphase == 2:
        plan_ffn_down(xin, out, nokeys, "fo")
    else:
        plan_ffn_down(xin, x1d.ap(), nokeys, "x1")
        plan_ln(x1d.ap(), lambda t4: [f"x1.{mg}.{t4}" for mg in range(16)])
        plan_attn_a()
        if nphase == 3:
            for i in range(16):
                P.op("pool", lambda e, i=i: e.dma_start(
                    out=out[i * 128:i * 128 + 128, :], in_=qac(i)),
                    reads=[f"qa{i}.{t}" for t in range(4)],
                    writes=[f"fo.{i}.0"], dma=True)
        elif nphase == 4:
            plan_attn_b(out, "fo")
        else:
            plan_attn_b(x2d.ap(), "x2")
            plan_ln(x2d.ap(), lambda t4: [f"x2.{mg}.{t4}" for mg in range(16)])
            plan_ffn_up(None)
            plan_ffn_down(x2d.ap(),
                          out, lambda mg, t4: [f"x2.{mg}.{t4}"], "fo")

    final = dict(P.cnt)
    nc._plan_cnt = final
    with nc.Block() as block:
        @block.sync
        def _(eng):
            P.run("sp", eng)
            for s in sem_names:
                if final[s] > 0:
                    eng.wait_ge(sems[s], final[s])

        @block.tensor
        def _(eng):
            P.run("pe", eng)

        @block.scalar
        def _(eng):
            P.run("act", eng)

        @block.vector
        def _(eng):
            P.run("dve", eng)

        @block.gpsimd
        def _(eng):
            P.run("pool", eng)

    es.close()
    return nc


def kernel(L0_attn_hidden_states, L0_attn_Wq, L0_attn_Wk, L0_attn_Wv,
           L0_attn_Wo, L0_attn_bo, L0_ffn_W1, L0_ffn_W2, L0_ffn_W3):
    x = np.asarray(L0_attn_hidden_states, dtype=np.float32)
    w1t = np.ascontiguousarray(np.asarray(L0_ffn_W1, np.float32).T)
    w3t = np.ascontiguousarray(np.asarray(L0_ffn_W3, np.float32).T)
    w2t = np.ascontiguousarray(np.asarray(L0_ffn_W2, np.float32).T)
    wq = np.ascontiguousarray(np.asarray(L0_attn_Wq, np.float32))
    wk = np.ascontiguousarray(np.asarray(L0_attn_Wk, np.float32))
    wv = np.ascontiguousarray(np.asarray(L0_attn_Wv, np.float32))
    wo = np.ascontiguousarray(np.asarray(L0_attn_Wo, np.float32))
    bo = np.asarray(L0_attn_bo, np.float32)
    bo_t = np.ascontiguousarray(bo.reshape(16, 128).T)

    hs = np.ascontiguousarray((_had(128) * (0.5 / np.sqrt(8192.0))).astype(np.float32))
    h64 = np.ascontiguousarray(_had(64).astype(ml_dtypes.bfloat16))
    sel = np.zeros((32, 2048), np.float32)
    selt = np.zeros((128, 512), np.float32)
    for c in range(16):
        for j in range(128):
            h = 2 * c + (1 if j >= 64 else 0)
            sel[h, 128 * c + j] = 1.0
            selt[j, 32 * c + h] = 1.0

    nc = build_program(NPHASE)
    in_maps = []
    for g in range(8):
        b, q = g // 4, g % 4
        xs = np.ascontiguousarray(x[b, q * 2048:(q + 1) * 2048, :].T)
        in_maps.append({
            "xin": xs, "w1t": w1t, "w3t": w3t, "w2t": w2t,
            "wq": wq, "wkd": wk, "wv": wv, "wo": wo, "bo_t": bo_t,
            "hs128": hs, "h64d": h64, "seld": sel, "seltd": selt,
        })
    res = run_bass_kernel_spmd(nc, in_maps, core_ids=list(range(8)))
    outp = np.empty((2, 8192, 2048), np.float32)
    for g in range(8):
        b, q = g // 4, g % 4
        outp[b, q * 2048:(q + 1) * 2048, :] = res.results[g]["out"].T
    return outp


# version 18
# speedup vs baseline: 1.0213x; 1.0201x over previous
import numpy as np
import ml_dtypes
import concourse.bass as bass
import concourse.mybir as mybir
from concourse.bass_utils import run_bass_kernel_spmd
from contextlib import ExitStack

F32 = mybir.dt.float32
BF16 = mybir.dt.bfloat16
U8 = mybir.dt.uint8
AF = mybir.ActivationFunctionType
ALU = mybir.AluOpType

NPHASE = 6
STRICT = False

D = 2048
T = 2048
HID = 8192
LN_EPS = 1e-5

XN_B = 0
QA_B = 65536
WST_B = 131072
XF_B = 163840
SGEV_B = 180224
SHR_B = 188416
STRIP_B = 201728
CST_B = 205824
ARENA_B = 212480

CC_B = WST_B + 24576
KVB_B = WST_B + 28736
KSM_B = WST_B + 30784
RECIP_B = XN_B + 16512
SELT_B = CST_B + 2560


def _had(n):
    h = np.array([[1.0]])
    while h.shape[0] < n:
        h = np.block([[h, h], [h, -h]])
    return h


def _xncov(b0, b1):
    return [f"xn{c}.{t}" for c in range(b0 // 4096, (b1 - 1) // 4096 + 1)
            for t in range(4)]


def build_program(nphase):
    nc = bass.Bass("TRN2", target_bir_lowering=False)

    xin = nc.dram_tensor("xin", [D, T], F32, kind="ExternalInput").ap()
    w1t = nc.dram_tensor("w1t", [D, HID], F32, kind="ExternalInput").ap()
    w3t = nc.dram_tensor("w3t", [D, HID], F32, kind="ExternalInput").ap()
    w2t = nc.dram_tensor("w2t", [HID, D], F32, kind="ExternalInput").ap()
    wq = nc.dram_tensor("wq", [D, D], F32, kind="ExternalInput").ap()
    wkd = nc.dram_tensor("wkd", [D, D], F32, kind="ExternalInput").ap()
    wv = nc.dram_tensor("wv", [D, D], F32, kind="ExternalInput").ap()
    wo = nc.dram_tensor("wo", [D, D], F32, kind="ExternalInput").ap()
    bo_t = nc.dram_tensor("bo_t", [128, 16], F32, kind="ExternalInput").ap()
    hs128 = nc.dram_tensor("hs128", [128, 128], F32, kind="ExternalInput").ap()
    h64d = nc.dram_tensor("h64d", [64, 64], BF16, kind="ExternalInput").ap()
    seld = nc.dram_tensor("seld", [32, 2048], F32, kind="ExternalInput").ap()
    seltd = nc.dram_tensor("seltd", [128, 512], F32, kind="ExternalInput").ap()

    ydr = nc.dram_tensor("ydr", [HID, D], BF16)
    w2s = nc.dram_tensor("w2s", [HID, D], BF16)
    hidd = nc.dram_tensor("hidd", [HID, T], BF16)
    x1d = nc.dram_tensor("x1d", [D, T], F32)
    x2d = nc.dram_tensor("x2d", [D, T], F32)
    cci = nc.dram_tensor("cci", [128, 1040], F32)
    cco = nc.dram_tensor("cco", [128, 1040], F32)
    out = nc.dram_tensor("out", [D, T], F32, kind="ExternalOutput").ap()

    es = ExitStack()
    arena = es.enter_context(nc.sbuf_tensor("arena", [128, ARENA_B], U8))
    a_f32 = arena.bitcast(F32).ap()
    a_bf = arena.bitcast(BF16).ap()

    def f32c(boff, rows, cols):
        assert boff % 4 == 0 and boff + 4 * cols <= ARENA_B
        return a_f32[0:rows, boff // 4: boff // 4 + cols]

    def bfc(boff, rows, cols):
        assert boff % 2 == 0 and boff + 2 * cols <= ARENA_B
        return a_bf[0:rows, boff // 2: boff // 2 + cols]

    ps = [es.enter_context(nc.psum_tensor(f"ps{i}", [128, 512], F32)) for i in range(8)]
    NDS = 8
    sem_names = ["epe", "eact", "edve", "epool"]
    for q in ("sp", "act", "pool"):
        sem_names += [f"d{q}{j}" for j in range(NDS)]
    sems = {s: es.enter_context(nc.semaphore(s)) for s in sem_names}

    ESEM = {"pe": "epe", "act": "eact", "dve": "edve", "pool": "epool"}

    class Plan:
        def __init__(self):
            self.ops = {e: [] for e in ("sp", "pe", "act", "dve", "pool")}
            self.cnt = {s: 0 for s in sems}
            self.wm = {e: {} for e in self.ops}
            self.lastw = {}
            self.readers = {}
            self.dma_idx = {"sp": 0, "act": 0, "pool": 0}

        def op(self, eng, fn, reads=(), writes=(), dma=False):
            need = {}

            def add(sc):
                if sc is not None and need.get(sc[0], 0) < sc[1]:
                    need[sc[0]] = sc[1]

            if STRICT == "all" or STRICT is True:
                for s, c in self.cnt.items():
                    if c > 0:
                        add((s, c))
            elif STRICT == "dma" and dma:
                for s, c in self.cnt.items():
                    if s[0] == "d" and c > 0:
                        add((s, c))
            elif STRICT == "compute" and not dma:
                for s in ("epe", "eact", "edve", "epool"):
                    if self.cnt[s] > 0:
                        add((s, self.cnt[s]))
            for k in reads:
                add(self.lastw.get(k))
            for k in writes:
                add(self.lastw.get(k))
                for item in (self.readers.get(k) or {}).items():
                    add(item)
            if dma:
                j = self.dma_idx[eng] % NDS
                self.dma_idx[eng] += 1
                sem = f"d{eng}{j}"
                if self.cnt[sem] > 0:
                    add((sem, self.cnt[sem]))
                self.cnt[sem] += 16
            else:
                sem = ESEM[eng]
                self.cnt[sem] += 1
            post = (sem, self.cnt[sem])
            waits = []
            for s, c in sorted(need.items()):
                if eng == "pe" and s == "epe":
                    continue
                if c <= self.wm[eng].get(s, 0):
                    continue
                waits.append((s, c))
                self.wm[eng][s] = c
            for k in writes:
                self.lastw[k] = post
                self.readers[k] = {}
            for k in reads:
                d = self.readers.setdefault(k, {})
                if d.get(post[0], 0) < post[1]:
                    d[post[0]] = post[1]
            self.ops[eng].append((waits, fn, post))

        def run(self, ename, eng):
            for waits, fn, post in self.ops[ename]:
                for s, c in waits:
                    eng.wait_ge(sems[s], c)
                inst = fn(eng)
                inst.then_inc(sems[post[0]], 16 if post[0][0] == "d" else 1)

    P = Plan()

    c_hs = f32c(CST_B, 128, 128)
    c_h64 = bfc(CST_B + 512, 64, 64)
    c_h64b = a_bf[64:128, (CST_B + 640) // 2:(CST_B + 640) // 2 + 64]
    c_1f = f32c(CST_B + 768, 128, 1)
    c_1b = bfc(CST_B + 772, 128, 1)
    c_1r = f32c(CST_B + 784, 1, 128)
    c_bo = f32c(CST_B + 1296, 128, 16)
    c_xsq = [bfc(CST_B + 1360, 128, 256), bfc(CST_B + 1872, 128, 256)]

    P.op("sp", lambda e: e.dma_start(out=c_hs, in_=hs128), writes=["c_hs"], dma=True)
    P.op("sp", lambda e: e.dma_start(out=c_h64, in_=h64d), writes=["c_h64"], dma=True)
    P.op("sp", lambda e: e.dma_start(out=c_h64b, in_=h64d), writes=["c_h64b"], dma=True)
    P.op("sp", lambda e: e.dma_start(out=c_bo, in_=bo_t), writes=["c_bo"], dma=True)
    P.op("pool", lambda e: e.memset(c_1f, 1.0), writes=["c_1f"])
    P.op("pool", lambda e: e.memset(c_1b, 1.0), writes=["c_1b"])
    P.op("pool", lambda e: e.memset(c_1r, 1.0), writes=["c_1r"])

    xfb = bfc(XF_B, 128, 4096)
    xfb3 = xfb.rearrange("p (c t) -> p c t", c=16)
    xlb = bfc(QA_B + 57344, 128, 4096)
    xlb3 = xlb.rearrange("p (c t) -> p c t", c=16)
    xf = f32c(XF_B, 128, 4096)
    sg = [f32c(SGEV_B + j * 2048, 128, 512) for j in range(4)]

    def strip(j):
        return f32c(STRIP_B + j * 1024, 1, 256)

    def xnc(c):
        return bfc(XN_B + c * 4096, 128, 2048)

    def qac(i):
        return bfc(QA_B + i * 4096, 128, 2048)

    def plan_ln(src_dram, src_keys, side=None, nside=0, tiles=range(8)):
        src_r = src_dram.rearrange("(c p) t -> p c t", p=128)
        for tt in tiles:
            for _ in range(nside):
                if side:
                    side.pop(0)()
            t0 = tt * 256
            t4 = tt // 2
            x3 = xfb3 if tt % 2 == 0 else xlb3
            xk = ((lambda c: f"xf{c // 2}") if tt % 2 == 0
                  else (lambda c: f"qa{14 + c // 8}.{(c // 2) % 4}"))
            XALL = sorted({xk(c) for c in range(16)})
            if side is not None:
                bs = [0, 1] if tt % 2 == 0 else [2, 3]
                bb = [2, 3] if tt % 2 == 0 else [0, 1]
            else:
                bs = [0, 1] if tt % 2 == 0 else [2, 3]
                bb = [4, 5] if tt % 2 == 0 else [0, 1]
            P.op("pool", lambda e, t0=t0, x3=x3: e.dma_start(
                out=x3, in_=src_r[:, :, t0:t0 + 256]),
                reads=src_keys(t4), writes=XALL, dma=True)
            def sumx(e, x3=x3, bs=bs):
                last = None
                for c in range(16):
                    last = e.matmul(ps[bs[0]][0:1, 0:256], c_1b, x3[:, c, :],
                                    start=(c == 0), stop=(c == 15),
                                    skip_group_check=True)
                return last
            P.op("pe", sumx, reads=XALL + ["c_1b"], writes=[f"ps{bs[0]}"])
            for c in range(16):
                xq = c_xsq[c % 2]
                xqk = f"xsq{c % 2}"
                P.op("act", lambda e, c=c, xq=xq, x3=x3: e.activation(
                    xq, x3[:, c, :], AF.Square),
                    reads=[xk(c)], writes=[xqk])
                P.op("pe", lambda e, c=c, xq=xq, bs=bs: e.matmul(
                    ps[bs[1]][0:1, 0:256], c_1b, xq,
                    start=(c == 0), stop=(c == 15), skip_group_check=True),
                    reads=[xqk, "c_1b"],
                    writes=[f"ps{bs[1]}"] if c in (0, 15) else [])
            P.op("dve", lambda e, bs=bs: e.tensor_scalar_mul(
                strip(0), ps[bs[0]][0:1, 0:256], 1.0 / 2048.0),
                reads=[f"ps{bs[0]}"], writes=["st0"])
            P.op("dve", lambda e, bs=bs: e.tensor_scalar_mul(
                strip(1), ps[bs[1]][0:1, 0:256], 1.0 / 2048.0),
                reads=[f"ps{bs[1]}"], writes=["st1"])
            P.op("dve", lambda e: e.tensor_tensor(strip(2), strip(0), strip(0), ALU.mult),
                 reads=["st0"], writes=["st2"])
            P.op("dve", lambda e: e.tensor_sub(strip(1), strip(1), strip(2)),
                 reads=["st1", "st2"], writes=["st1"])
            P.op("dve", lambda e: e.tensor_scalar_add(strip(1), strip(1), LN_EPS),
                 reads=["st1"], writes=["st1"])
            P.op("act", lambda e: e.activation(strip(2), strip(1), AF.Sqrt),
                 reads=["st1"], writes=["st2"])
            P.op("dve", lambda e: e.reciprocal(strip(3), strip(2)),
                 reads=["st2"], writes=["st3"])
            P.op("dve", lambda e: e.tensor_tensor(strip(1), strip(0), strip(3), ALU.mult),
                 reads=["st0", "st3"], writes=["st1"])
            P.op("pe", lambda e, bb=bb: e.matmul(ps[bb[0]][:, 0:256], c_1r, strip(3),
                                                 start=True, stop=True,
                                                 skip_group_check=True),
                 reads=["st3", "c_1r"], writes=[f"ps{bb[0]}"])
            P.op("pe", lambda e, bb=bb: e.matmul(ps[bb[1]][:, 0:256], c_1r, strip(1),
                                                 start=True, stop=True,
                                                 skip_group_check=True),
                 reads=["st1", "c_1r"], writes=[f"ps{bb[1]}"])
            for c in range(16):
                sgj = c % 2
                P.op("dve", lambda e, c=c, sgj=sgj, x3=x3, bb=bb: e.tensor_tensor(
                    sg[sgj][:, 0:256], x3[:, c, :], ps[bb[0]][:, 0:256], ALU.mult),
                    reads=[xk(c), f"ps{bb[0]}"], writes=[f"sg{sgj}"])
                P.op("dve", lambda e, c=c, sgj=sgj, t0=t0, bb=bb: e.tensor_tensor(
                    xnc(c)[:, t0:t0 + 256], sg[sgj][:, 0:256], ps[bb[1]][:, 0:256],
                    ALU.subtract),
                    reads=[f"sg{sgj}", f"ps{bb[1]}"], writes=[f"xn{c}.{t4}"])

    def make_p0_thunks():
        NRING = 16
        DEPTH = 12

        def ring(j):
            return bfc(QA_B + 16384 + j * 2048, 128, 1024)

        def ringk(j):
            return [f"p0r{j}"]

        p0outs = ([bfc(XF_B + 8192 + k * 2048, 128, 1024) for k in range(4)] +
                  [bfc(QA_B + 16384 + (16 + k) * 2048, 128, 1024) for k in range(4)])
        p0oks = ([[f"xf{8 + 2 * k}", f"xf{9 + 2 * k}"] for k in range(4)] +
                 [[f"p0r{16 + k}"] for k in range(4)])

        def ringf(j):
            return f32c(QA_B + 16384 + j * 4096, 128, 1024)

        loads1, comps1, loads2, comps2 = [], [], [], []
        for cp in range(64):
            for mh in range(2):
                i = cp * 2 + mh
                s = i % 8
                ins = ringf(s)
                ik = [f"p0r{2 * s}", f"p0r{2 * s + 1}"]
                outs = p0outs[i % 8]
                ok = p0oks[i % 8]

                def ld(cp=cp, mh=mh, ins=ins, ik=ik):
                    P.op("act", lambda e: e.dma_start(
                        out=ins,
                        in_=w2t[cp * 128:cp * 128 + 128, mh * 1024:mh * 1024 + 1024]),
                        writes=ik, dma=True)

                def cm(cp=cp, mh=mh, i=i, ins=ins, outs=outs, ik=ik, ok=ok):
                    for q in range(2):
                        b = 4 + 2 * (i % 2) + q
                        P.op("pe", lambda e, q=q, b=b: e.matmul(
                            ps[b][:, :], c_hs, ins[:, q * 512:q * 512 + 512],
                            start=True, stop=True, skip_group_check=True),
                            reads=ik + ["c_hs"], writes=[f"ps{b}"])
                        P.op("dve", lambda e, q=q, b=b: e.tensor_copy(
                            outs[:, q * 512:q * 512 + 512], ps[b][:, :]),
                            reads=[f"ps{b}"], writes=ok)
                    P.op("sp", lambda e: e.dma_start(
                        out=ydr.ap()[cp * 128:cp * 128 + 128,
                                     mh * 1024:mh * 1024 + 1024],
                        in_=outs),
                        reads=ok, writes=[f"ydr{cp}"], dma=True)
                loads1.append(ld)
                comps1.append(cm)
        y_r2 = ydr.ap().rearrange("(c p) m -> p c m", p=128)
        ws_r2 = w2s.ap().rearrange("(c p) m -> p c m", p=128)
        first = [True]
        for dp in range(64):
            for mh in range(2):
                i = dp * 2 + mh
                s = i % NRING
                ins = ring(s)
                ik = ringk(s)
                outs = p0outs[i % 8]
                ok = p0oks[i % 8]

                def ld(dp=dp, mh=mh, ins=ins, ik=ik):
                    rd = ik + ([f"ydr{cp}" for cp in range(64)]
                               if first[0] else [f"ydr{0}"])
                    first[0] = False
                    P.op("sp", lambda e: e.dma_start(
                        out=ins,
                        in_=y_r2[2 * dp:2 * dp + 2, :, mh * 1024:mh * 1024 + 1024]),
                        reads=rd, writes=ik, dma=True)

                def cm(dp=dp, mh=mh, i=i, ins=ins, outs=outs, ik=ik, ok=ok):
                    for q in range(2):
                        b = 4 + 2 * (i % 2) + q
                        def mmp(e, q=q, b=b, ins=ins):
                            e.matmul(
                                ps[b][0:64, :], c_h64,
                                ins[0:64, q * 512:q * 512 + 512],
                                start=True, stop=True, skip_group_check=True)
                            return e.matmul(
                                ps[b][64:128, :], c_h64b,
                                ins[64:128, q * 512:q * 512 + 512],
                                start=True, stop=True, skip_group_check=True)
                        P.op("pe", mmp, reads=ik + ["c_h64", "c_h64b"],
                             writes=[f"ps{b}"])
                        P.op("dve", lambda e, q=q, b=b: e.tensor_copy(
                            outs[:, q * 512:q * 512 + 512], ps[b][:, :]),
                            reads=[f"ps{b}"], writes=ok)
                    P.op("sp", lambda e: e.dma_start(
                        out=ws_r2[2 * dp:2 * dp + 2, :, mh * 1024:mh * 1024 + 1024],
                        in_=outs),
                        reads=ok, writes=["w2s"], dma=True)
                loads2.append(ld)
                comps2.append(cm)

        thunks = []
        for loads, comps, dep in ((loads1, comps1, 6), (loads2, comps2, DEPTH)):
            n = len(loads)
            for k in range(n + dep):
                def th(k=k, loads=loads, comps=comps, n=n, dep=dep):
                    if k < n:
                        loads[k]()
                    if k >= dep:
                        comps[k - dep]()
                thunks.append(th)
        return thunks

    def plan_ffn_up(side):
        w1r = w1t.rearrange("(kc p) h -> p kc h", p=128)
        w3r = w3t.rearrange("(kc p) h -> p kc h", p=128)
        pos = [0]
        pend = []
        for hs in range(32):
            while pend:
                pend.pop(0)()
            for _ in range(7):
                if side:
                    side.pop(0)()
            slot = hs % 2
            wslab = bfc(WST_B + slot * 16384, 128, 8192)
            ka, kb = f"w{slot}a", f"w{slot}b"
            P.op("pool", lambda e, hs=hs, wslab=wslab: e.dma_start(
                out=wslab[:, 0:4096].rearrange("p (kc h) -> p kc h", kc=16),
                in_=w1r[:, :, hs * 256:hs * 256 + 256]),
                writes=[ka], dma=True)
            P.op("pool", lambda e, hs=hs, wslab=wslab: e.dma_start(
                out=wslab[:, 4096:8192].rearrange("p (kc h) -> p kc h", kc=16),
                in_=w3r[:, :, hs * 256:hs * 256 + 256]),
                writes=[kb], dma=True)
            W = 2 if side is not None else 4
            npos = (4 if side is not None else 8) // W
            for hc in range(2):
                hcg = hs * 2 + hc
                qslot = hcg % 4
                slab = qac(qslot)
                for w0 in range(0, 4, W):
                    gb = {}
                    for off, wkk, mat in ((0, ka, "g"), (4096, kb, "l")):
                        p = pos[0]
                        pos[0] = (pos[0] + 1) % npos
                        banks = [p * W + j for j in range(W)]
                        gb[mat] = banks

                        def mmg(e, off=off, banks=banks, hc=hc, w0=w0, wslab=wslab):
                            last = None
                            for kc in range(16):
                                o = off + kc * 256 + hc * 128
                                for j, b in enumerate(banks):
                                    last = e.matmul(
                                        ps[b][:, :], wslab[:, o:o + 128],
                                        xnc(kc)[:, (w0 + j) * 512:(w0 + j) * 512 + 512],
                                        start=(kc == 0), stop=(kc == 15),
                                        skip_group_check=True)
                            return last
                        P.op("pe", mmg,
                             reads=[wkk] + [f"xn{kc}.{w0 + j}" for kc in range(16)
                                            for j in range(W)],
                             writes=[f"ps{b}" for b in banks])
                    for j in range(W):
                        t4 = w0 + j
                        bg, bl = gb["g"][j], gb["l"][j]
                        sgj = t4 % 4
                        P.op("act", lambda e, bg=bg, sgj=sgj: e.activation(
                            sg[sgj], ps[bg][:, :], AF.Silu),
                            reads=[f"ps{bg}"], writes=[f"sg{sgj}"])
                        P.op("dve", lambda e, bl=bl, sgj=sgj, slab=slab, t4=t4,
                             qslot=qslot: e.tensor_tensor(
                            slab[:, t4 * 512:t4 * 512 + 512], sg[sgj], ps[bl][:, :],
                            ALU.mult),
                            reads=[f"sg{sgj}", f"ps{bl}"], writes=[f"qa{qslot}.{t4}"])
                def st(hcg=hcg, slab=slab, qslot=qslot):
                    P.op("sp", lambda e: e.dma_start(
                        out=hidd.ap()[hcg * 128:hcg * 128 + 128, :], in_=slab),
                        reads=[f"qa{qslot}.{t}" for t in range(4)],
                        writes=[f"hid{hcg}"], dma=True)
                pend.append(st)
        while pend:
            pend.pop(0)()

    def plan_ffn_down(resid_dram, out_dram, resid_keys, okey, mid=None):
        hid_r = hidd.ap().rearrange("(hc p) t -> p hc t", p=128)
        ws_r2 = w2s.ap().rearrange("(hc p) m -> p hc m", p=128)

        def cover(hclo, hchi):
            keys = []
            for hc in range(hclo, hchi):
                if hc < 32:
                    keys += [f"xn{hc // 2}.{t}" for t in range(4)]
                else:
                    keys += [f"qa{(hc - 32) // 2}.{t}" for t in range(4)]
            return sorted(set(keys))

        for th in range(2):
            if th == 1 and mid is not None:
                mid()
            for g in range(8):
                dst = a_bf[:, g * 8192: g * 8192 + 8192]
                P.op("sp", lambda e, th=th, g=g, dst=dst: e.dma_start(
                    out=dst.rearrange("p (hc t) -> p hc t", hc=8),
                    in_=hid_r[:, g * 8:g * 8 + 8, th * 1024:th * 1024 + 1024]),
                    reads=[f"hid{hc}" for hc in range(g * 8, g * 8 + 8)] + ["w2s"],
                    writes=cover(g * 8, g * 8 + 8), dma=True)
            for ms in range(8):
                banks = [4 * (ms % 2) + j for j in range(4)]
                for sub in range(2):
                    wsl = (ms * 2 + sub) % 2
                    wslab = bfc(WST_B + wsl * 16384, 128, 8192)
                    wkeys = [f"w{wsl}a", f"w{wsl}b"]
                    rd = ["w2s"] if (th == 0 and ms == 0 and sub == 0) else []
                    P.op("act", lambda e, ms=ms, sub=sub, wslab=wslab: e.dma_start(
                        out=wslab.rearrange("p (hc m) -> p hc m", hc=32),
                        in_=ws_r2[:, sub * 32:sub * 32 + 32, ms * 256:ms * 256 + 256]),
                        reads=rd, writes=wkeys, dma=True)

                    for part in range(4):
                        def mmh(e, sub=sub, part=part, wslab=wslab, banks=banks):
                            last = None
                            for hcl in range(part * 8, part * 8 + 8):
                                hc = sub * 32 + hcl
                                hbuf = a_bf[:, hc * 1024: hc * 1024 + 1024]
                                for mc in range(2):
                                    for t2 in range(2):
                                        last = e.matmul(
                                            ps[banks[mc * 2 + t2]][:, :],
                                            wslab[:, hcl * 256 + mc * 128:
                                                  hcl * 256 + mc * 128 + 128],
                                            hbuf[:, t2 * 512:t2 * 512 + 512],
                                            start=(hc == 0), stop=(hc == 63),
                                            skip_group_check=True)
                            return last
                        P.op("pe", mmh,
                             reads=wkeys + cover(sub * 32 + part * 8,
                                                 sub * 32 + part * 8 + 8),
                             writes=[f"ps{b}" for b in banks] if part in (0, 3)
                                    else [])
                for mc in range(2):
                    for t2 in range(2):
                        j = mc * 2 + t2
                        b = banks[j]
                        mg = ms * 2 + mc
                        t4 = th * 2 + t2
                        r0, c0 = mg * 128, t4 * 512
                        xt = xf[:, j * 512:j * 512 + 512]
                        xk = [f"xf{2 * j}", f"xf{2 * j + 1}"]
                        P.op("act", lambda e, r0=r0, c0=c0, xt=xt, mg=mg, t4=t4:
                             e.dma_start(out=xt,
                                         in_=resid_dram[r0:r0 + 128, c0:c0 + 512]),
                             reads=resid_keys(mg, t4), writes=xk, dma=True)
                        P.op("dve", lambda e, b=b, xt=xt, j=j: e.tensor_tensor(
                            sg[j], ps[b][:, :], xt, ALU.add),
                            reads=[f"ps{b}"] + xk, writes=[f"sg{j}"])
                        P.op("sp", lambda e, r0=r0, c0=c0, j=j: e.dma_start(
                            out=out_dram[r0:r0 + 128, c0:c0 + 512], in_=sg[j]),
                            reads=[f"sg{j}"], writes=[f"{okey}.{mg}.{t4}"], dma=True)

    def plan_attn_a():
        wqr = wq.rearrange("(kc p) i -> p kc i", p=128)
        wkr = wkd.rearrange("(kc p) i -> p kc i", p=128)
        wvr = wv.rearrange("(kc p) i -> p kc i", p=128)
        ktm_s = [bfc(SHR_B + 8192, 128, 256), bfc(SHR_B + 8704, 128, 256)]
        vtm_s = [bfc(SHR_B + 9216, 128, 256), bfc(SHR_B + 9728, 128, 256)]
        wcnt = [0]

        def wkey(j):
            return f"w{j // 2}{'ab'[j % 2]}"

        def wload(si, src):
            j = wcnt[0] % 3
            wcnt[0] += 1
            wsl = bfc(WST_B + j * 8192, 128, 4096)
            P.op("pool", lambda e, si=si, src=src, wsl=wsl: e.dma_start(
                out=wsl.rearrange("p (kc i) -> p kc i", kc=16),
                in_=src[:, :, si * 256:si * 256 + 256]),
                writes=[wkey(j)], dma=True)
            return j

        for si in range(8):
            jk = wload(si, wkr)
            jv = wload(si, wvr)
            for tc in range(16):
                par = tc % 2
                ktm, vtm = ktm_s[par], vtm_s[par]
                for jm, dst, dk, pb, relu in ((jk, ktm, f"ktm{par}", 2 + par, True),
                                              (jv, vtm, f"vtm{par}", 4 + par, False)):
                    wsl = bfc(WST_B + jm * 8192, 128, 4096)
                    pk = f"ps{pb}"
                    def kvp(e, tc=tc, wsl=wsl, pb=pb):
                        last = None
                        for kc in range(16):
                            last = e.matmul(
                                ps[pb][:, 0:256],
                                xnc(kc)[:, tc * 128:tc * 128 + 128],
                                wsl[:, kc * 256:kc * 256 + 256],
                                start=(kc == 0), stop=(kc == 15),
                                skip_group_check=True)
                        return last
                    P.op("pe", kvp,
                         reads=[wkey(jm)] + [f"xn{kc}.{tc // 4}" for kc in range(16)],
                         writes=[pk])
                    if relu:
                        P.op("act", lambda e, dst=dst, pb=pb: e.activation(
                            dst, ps[pb][:, 0:256], AF.Relu),
                            reads=[pk], writes=[dk])
                    else:
                        P.op("act", lambda e, dst=dst, pb=pb: e.copy(
                            dst, ps[pb][:, 0:256]),
                            reads=[pk], writes=[dk])

                def kva(e, si=si, tc=tc, ktm=ktm, vtm=vtm):
                    last = None
                    for hh in range(4):
                        h = si * 4 + hh
                        ch, p_ = h // 2, h % 2
                        bank = 6 + (ch // 8)
                        last = e.matmul(
                            ps[bank][p_ * 64:p_ * 64 + 64,
                                     (ch % 8) * 64:(ch % 8) * 64 + 64],
                            ktm[:, hh * 64:hh * 64 + 64],
                            vtm[:, hh * 64:hh * 64 + 64],
                            start=(tc == 0), stop=(tc == 15),
                            skip_group_check=True)
                    return last
                wr = ["ps6", "ps7"] if (si == 0 and tc == 0) or (si == 7 and tc == 15) else []
                P.op("pe", kva, reads=[f"ktm{par}", f"vtm{par}"], writes=wr)

                def ksa(e, si=si, tc=tc, ktm=ktm):
                    last = None
                    for jc in range(2):
                        icg = si * 2 + jc
                        last = e.matmul(
                            ps[0][:, icg:icg + 1],
                            ktm[:, jc * 128:jc * 128 + 128], c_1b,
                            start=(tc == 0), stop=(tc == 15),
                            skip_group_check=True)
                    return last
                wr = ["ps0"] if (si == 0 and tc == 0) or (si == 7 and tc == 15) else []
                P.op("pe", ksa, reads=[f"ktm{par}", "c_1b"], writes=wr)
        ccs = f32c(CC_B, 128, 1040)
        cc2 = ccs
        P.op("dve", lambda e: e.tensor_copy(ccs[:, 0:512], ps[6][:, :]),
             reads=["ps6"], writes=["ccs", "w1b"])
        P.op("dve", lambda e: e.tensor_copy(ccs[:, 512:1024], ps[7][:, :]),
             reads=["ps7"], writes=["ccs"])
        P.op("dve", lambda e: e.tensor_copy(ccs[:, 1024:1040], ps[0][:, 0:16]),
             reads=["ps0"], writes=["ccs"])
        P.op("sp", lambda e: e.dma_start(out=cci.ap(), in_=ccs),
             reads=["ccs"], writes=["cci"], dma=True)
        P.op("pool", lambda e: e.collective_compute(
            "AllReduce", ALU.add,
            replica_groups=[[0, 1, 2, 3], [4, 5, 6, 7]],
            ins=[cci.ap().opt()], outs=[cco.ap().opt()]),
            reads=["cci"], writes=["cco"])
        P.op("sp", lambda e: e.dma_start(out=cc2, in_=cco.ap()),
             reads=["cco"], writes=["cc2", "ccs", "w1b"], dma=True)
        for si in range(8):
            jq = wload(si, wqr)
            wq_s = bfc(WST_B + jq * 8192, 128, 4096)
            for ic in range(2):
                icg = si * 2 + ic
                for t4 in range(4):
                    b = (ic * 4 + t4) % 4
                    def qmm(e, ic=ic, t4=t4, b=b, wq_s=wq_s):
                        last = None
                        for kc in range(16):
                            o = kc * 256 + ic * 128
                            last = e.matmul(
                                ps[b][:, :], wq_s[:, o:o + 128],
                                xnc(kc)[:, t4 * 512:t4 * 512 + 512],
                                start=(kc == 0), stop=(kc == 15),
                                skip_group_check=True)
                        return last
                    P.op("pe", qmm,
                         reads=[wkey(jq)] + [f"xn{kc}.{t4}" for kc in range(16)],
                         writes=[f"ps{b}"])
                    P.op("act", lambda e, icg=icg, t4=t4, b=b: e.activation(
                        qac(icg)[:, t4 * 512:t4 * 512 + 512], ps[b][:, :], AF.Relu),
                        reads=[f"ps{b}"], writes=[f"qa{icg}.{t4}"])
        kvb = bfc(KVB_B, 128, 1024)
        P.op("dve", lambda e: e.tensor_copy(kvb[:, :], cc2[:, 0:1024]),
             reads=["cc2", "w1b"], writes=["kvb"])
        selt_s = f32c(SELT_B, 128, 512)
        P.op("sp", lambda e: e.dma_start(out=selt_s, in_=seltd),
             writes=["selt"], dma=True)
        for ic in range(16):
            P.op("dve", lambda e, ic=ic: e.tensor_scalar(
                bfc(KSM_B + ic * 64, 128, 32),
                selt_s[:, ic * 32:ic * 32 + 32],
                cc2[:, 1024 + ic:1024 + ic + 1], None, ALU.mult),
                reads=["selt", "cc2", "w1b"], writes=[f"ksm{ic}"])

    def plan_attn_b(out_dram, okey):
        kvb = bfc(KVB_B, 128, 1024)
        recip = f32c(RECIP_B, 32, 2048)
        sel_s = f32c(SHR_B, 32, 2048)
        rb = [f32c(SHR_B + 8192, 128, 512), f32c(SHR_B + 10240, 128, 512)]
        P.op("sp", lambda e: e.dma_start(out=sel_s, in_=seld),
             writes=["sel"], dma=True)
        for t4 in range(4):
            b = t4 % 2
            def nmm(e, t4=t4, b=b):
                last = None
                for ic in range(16):
                    last = e.matmul(
                        ps[b][0:32, :], bfc(KSM_B + ic * 64, 128, 32),
                        qac(ic)[:, t4 * 512:t4 * 512 + 512],
                        start=(ic == 0), stop=(ic == 15), skip_group_check=True)
                return last
            P.op("pe", nmm,
                 reads=[f"ksm{ic}" for ic in range(16)] + ["w1b"] +
                       [f"qa{ic}.{t4}" for ic in range(16)],
                 writes=[f"ps{b}"])
            P.op("dve", lambda e, b=b: e.tensor_scalar_max(
                sg[0][0:32, :], ps[b][0:32, :], 1e-6),
                reads=[f"ps{b}"], writes=["sg0"])
            P.op("dve", lambda e, t4=t4: e.reciprocal(
                recip[:, t4 * 512:t4 * 512 + 512], sg[0][0:32, :]),
                reads=["sg0"],
                writes=[f"recip{t4}"] + (_xncov(RECIP_B, RECIP_B + 8192)
                                         if t4 == 0 else []))
        for ic in range(16):
            for t4 in range(4):
                brb = 2 + (t4 % 2)
                ba = 4 + (t4 % 2)
                P.op("pe", lambda e, ic=ic, t4=t4, brb=brb: e.matmul(
                    ps[brb][:, :], sel_s[:, ic * 128:ic * 128 + 128],
                    recip[:, t4 * 512:t4 * 512 + 512],
                    start=True, stop=True, skip_group_check=True),
                    reads=["sel", f"recip{t4}"], writes=[f"ps{brb}"])
                P.op("act", lambda e, t4=t4, brb=brb: e.copy(rb[t4 % 2], ps[brb][:, :]),
                     reads=[f"ps{brb}"],
                     writes=[f"rb{t4 % 2}"] +
                            (["ktm0", "ktm1", "vtm0", "vtm1"] if t4 % 2 == 0 else []))

                def amm(e, ic=ic, t4=t4, ba=ba):
                    qs = qac(ic)
                    e.matmul(ps[ba][0:64, :], kvb[0:64, ic * 64:ic * 64 + 64],
                             qs[0:64, t4 * 512:t4 * 512 + 512],
                             start=True, stop=True, skip_group_check=True)
                    return e.matmul(ps[ba][64:128, :],
                                    kvb[64:128, ic * 64:ic * 64 + 64],
                                    qs[64:128, t4 * 512:t4 * 512 + 512],
                                    start=True, stop=True, skip_group_check=True)
                P.op("pe", amm, reads=["kvb", "w1b", f"qa{ic}.{t4}"], writes=[f"ps{ba}"])
                P.op("dve", lambda e, ic=ic, t4=t4, ba=ba: e.tensor_tensor(
                    qac(ic)[:, t4 * 512:t4 * 512 + 512], ps[ba][:, :], rb[t4 % 2],
                    ALU.mult),
                    reads=[f"ps{ba}", f"rb{t4 % 2}"], writes=[f"qa{ic}.{t4}"])
        wor = wo.rearrange("(ic p) m -> p ic m", p=128)
        for msl in range(8):
            slot = msl % 2
            wslab = bfc(WST_B + slot * 8192, 128, 4096)
            wkeys = [f"w0{'ab'[slot]}"]
            P.op("pool", lambda e, msl=msl, wslab=wslab: e.dma_start(
                out=wslab.rearrange("p (ic m) -> p ic m", ic=16),
                in_=wor[:, :, msl * 256:msl * 256 + 256]),
                writes=wkeys, dma=True)
            for mc in range(2):
                mg = msl * 2 + mc
                for t4 in range(4):
                    b = 6 + (t4 % 2)
                    def womm(e, mc=mc, t4=t4, b=b, wslab=wslab):
                        last = None
                        for ic in range(16):
                            o = ic * 256 + mc * 128
                            last = e.matmul(
                                ps[b][:, :], wslab[:, o:o + 128],
                                qac(ic)[:, t4 * 512:t4 * 512 + 512],
                                start=(ic == 0), stop=(ic == 15),
                                skip_group_check=True)
                        return last
                    P.op("pe", womm,
                         reads=wkeys + [f"qa{ic}.{t4}" for ic in range(16)],
                         writes=[f"ps{b}"])
                    j = (mc * 4 + t4) % 2
                    xt = xf[:, j * 512:j * 512 + 512]
                    xk = [f"xf{2 * j}", f"xf{2 * j + 1}"]
                    P.op("act", lambda e, mg=mg, t4=t4, xt=xt: e.dma_start(
                        out=xt,
                        in_=x1d.ap()[mg * 128:mg * 128 + 128, t4 * 512:t4 * 512 + 512]),
                        reads=[f"x1.{mg}.{t4}"], writes=xk, dma=True)
                    P.op("act", lambda e, b=b, mg=mg, j=j: e.activation(
                        sg[j], ps[b][:, :], AF.Identity, bias=c_bo[:, mg:mg + 1]),
                        reads=[f"ps{b}", "c_bo"], writes=[f"sg{j}"])
                    P.op("dve", lambda e, j=j, xt=xt: e.tensor_tensor(
                        sg[2 + j], sg[j], xt, ALU.add),
                        reads=[f"sg{j}"] + xk, writes=[f"sg{2 + j}"])
                    P.op("sp", lambda e, mg=mg, t4=t4, j=j: e.dma_start(
                        out=out_dram[mg * 128:mg * 128 + 128,
                                     t4 * 512:t4 * 512 + 512],
                        in_=sg[2 + j]),
                        reads=[f"sg{2 + j}"], writes=[f"{okey}.{mg}.{t4}"], dma=True)

    nokeys = lambda *a: []
    p0 = make_p0_thunks()
    plan_ln(xin, nokeys, side=p0, nside=8)
    plan_ffn_up(p0)
    while p0:
        p0.pop(0)()
    if nphase == 1:
        for i in range(16):
            P.op("pool", lambda e, i=i: e.dma_start(
                out=out[i * 128:i * 128 + 128, :],
                in_=w2s.ap()[i * 128:i * 128 + 128, :]),
                reads=["w2s"], writes=[f"fo.{i}.0"], dma=True)
    elif nphase == 2:
        plan_ffn_down(xin, out, nokeys, "fo")
    else:
        plan_ffn_down(xin, x1d.ap(), nokeys, "x1")
        plan_ln(x1d.ap(), lambda t4: [f"x1.{mg}.{t4}" for mg in range(16)])
        plan_attn_a()
        if nphase == 3:
            for i in range(16):
                P.op("pool", lambda e, i=i: e.dma_start(
                    out=out[i * 128:i * 128 + 128, :], in_=qac(i)),
                    reads=[f"qa{i}.{t}" for t in range(4)],
                    writes=[f"fo.{i}.0"], dma=True)
        elif nphase == 4:
            plan_attn_b(out, "fo")
        else:
            plan_attn_b(x2d.ap(), "x2")
            plan_ln(x2d.ap(), lambda t4: [f"x2.{mg}.{t4}" for mg in range(16)])
            plan_ffn_up(None)
            plan_ffn_down(x2d.ap(),
                          out, lambda mg, t4: [f"x2.{mg}.{t4}"], "fo")

    final = dict(P.cnt)
    nc._plan_cnt = final
    with nc.Block() as block:
        @block.sync
        def _(eng):
            P.run("sp", eng)
            for s in sem_names:
                if final[s] > 0:
                    eng.wait_ge(sems[s], final[s])

        @block.tensor
        def _(eng):
            P.run("pe", eng)

        @block.scalar
        def _(eng):
            P.run("act", eng)

        @block.vector
        def _(eng):
            P.run("dve", eng)

        @block.gpsimd
        def _(eng):
            P.run("pool", eng)

    es.close()
    return nc


def kernel(L0_attn_hidden_states, L0_attn_Wq, L0_attn_Wk, L0_attn_Wv,
           L0_attn_Wo, L0_attn_bo, L0_ffn_W1, L0_ffn_W2, L0_ffn_W3):
    x = np.asarray(L0_attn_hidden_states, dtype=np.float32)
    w1t = np.ascontiguousarray(np.asarray(L0_ffn_W1, np.float32).T)
    w3t = np.ascontiguousarray(np.asarray(L0_ffn_W3, np.float32).T)
    w2t = np.ascontiguousarray(np.asarray(L0_ffn_W2, np.float32).T)
    wq = np.ascontiguousarray(np.asarray(L0_attn_Wq, np.float32))
    wk = np.ascontiguousarray(np.asarray(L0_attn_Wk, np.float32))
    wv = np.ascontiguousarray(np.asarray(L0_attn_Wv, np.float32))
    wo = np.ascontiguousarray(np.asarray(L0_attn_Wo, np.float32))
    bo = np.asarray(L0_attn_bo, np.float32)
    bo_t = np.ascontiguousarray(bo.reshape(16, 128).T)

    hs = np.ascontiguousarray((_had(128) * (0.5 / np.sqrt(8192.0))).astype(np.float32))
    h64 = np.ascontiguousarray(_had(64).astype(ml_dtypes.bfloat16))
    sel = np.zeros((32, 2048), np.float32)
    selt = np.zeros((128, 512), np.float32)
    for c in range(16):
        for j in range(128):
            h = 2 * c + (1 if j >= 64 else 0)
            sel[h, 128 * c + j] = 1.0
            selt[j, 32 * c + h] = 1.0

    nc = build_program(NPHASE)
    in_maps = []
    for g in range(8):
        b, q = g // 4, g % 4
        xs = np.ascontiguousarray(x[b, q * 2048:(q + 1) * 2048, :].T)
        in_maps.append({
            "xin": xs, "w1t": w1t, "w3t": w3t, "w2t": w2t,
            "wq": wq, "wkd": wk, "wv": wv, "wo": wo, "bo_t": bo_t,
            "hs128": hs, "h64d": h64, "seld": sel, "seltd": selt,
        })
    res = run_bass_kernel_spmd(nc, in_maps, core_ids=list(range(8)))
    outp = np.empty((2, 8192, 2048), np.float32)
    for g in range(8):
        b, q = g // 4, g % 4
        outp[b, q * 2048:(q + 1) * 2048, :] = res.results[g]["out"].T
    return outp
